# version 17
# baseline (speedup 1.0000x reference)
import os
os.environ.setdefault("TRN_ALLOW_ICI_ALL_TO_ALL", "1")
import math
from contextlib import ExitStack

import numpy as np
import ml_dtypes

import concourse.bass as bass
import concourse.mybir as mybir
from concourse.bass_utils import run_bass_kernel_spmd

F32 = mybir.dt.float32
BF16 = mybir.dt.bfloat16
AF = mybir.ActivationFunctionType
ALU = mybir.AluOpType


class _Chain:

    def __init__(self, eng, ch_sem):
        self.e, self.ch, self.cnt, self.last = eng, ch_sem, 0, None

    def __call__(self, make, inc=None):
        if self.last is not None:
            self.e.wait_ge(*self.last)
        inst = make()
        if inc is None:
            self.cnt += 1
            inst.then_inc(self.ch, 1)
            self.last = (self.ch, self.cnt)
        else:
            s, absval, n = inc
            inst.then_inc(s, n)
            self.last = (s, absval)
        return inst


SEQ, DM, NH, HD, NC = 8192, 2048, 16, 128, 8
QT, KT = 512, 128
NQT = SEQ // QT
NCH = DM // 128
TPC = SEQ // NC
NKP = SEQ // KT
EPS = 1e-5
RSC = 1.0 / math.sqrt(HD)
CUT = 120.0
W1WIN = 14
NEG = -1.0e30
BCLAG = 5

SLOT_HEADS = [[15 - c for c in range(NC)], list(range(NC))]
SLOPES = [1.0 / 2.0 ** (h / 2.0) for h in range(NH)]
TAU_ORDER = [t for t in range(NQT) if t % 2 == 0] + \
            [t for t in range(NQT) if t % 2 == 1]


def _mk_windows():
    wins, it = [], 0
    for tau in TAU_ORDER:
        for s in (0, 1):
            nf = 4 * tau + 4
            n = nf if s == 0 else min(nf, W1WIN)
            wins.append(dict(s=s, tau=tau, kpos=list(range(nf - n, nf)),
                             it0=it, n=n))
            it += n
    return wins, it


WINDOWS, NITER = _mk_windows()
NWIN = len(WINDOWS)
NWIN_EVEN = NWIN // 2
ITERS = []
for _w, _win in enumerate(WINDOWS):
    for _j, _kp in enumerate(_win["kpos"]):
        ITERS.append(dict(w=_w, j=_j, kp=_kp, s=_win["s"], tau=_win["tau"],
                          first=(_j == 0), last=(_j == _win["n"] - 1)))
APD = 4



def _alibi_tiles(slope):
    dk = np.arange(KT, dtype=np.float64)[:, None]
    dq = np.arange(QT, dtype=np.float64)[None, :]
    t = np.zeros((5, KT, QT), dtype=np.float32)
    t[0] = np.exp(slope * (dk - dq - 128.0)).astype(np.float32)
    for v in range(4):
        arg = slope * (dk + 128.0 * v - dq)
        t[1 + v] = np.where(dk + 128 * v <= dq, np.exp(arg), 0.0).astype(np.float32)
    return t


def _bias_table(c):
    tbl = np.zeros((128, NITER), dtype=np.float32)
    for w in WINDOWS:
        slope = SLOPES[SLOT_HEADS[w["s"]][c]]
        tau = w["tau"]
        for j, kp in enumerate(w["kpos"]):
            if kp >= 4 * tau:
                v = 0.0
            else:
                mind = 512.0 * tau - 128.0 * kp - 128.0
                v = NEG if slope * mind > CUT else slope * (128.0 * kp + 128.0 - 512.0 * tau)
            tbl[:, w["it0"] + j] = v
    return tbl


def host_prepare(inputs):
    x = np.asarray(inputs["L0_attn_x"], dtype=np.float32).reshape(SEQ, DM)
    Wqkv = np.asarray(inputs["L0_attn_Wqkv"], dtype=np.float32)
    bqkv = np.asarray(inputs["L0_attn_bqkv"], dtype=np.float32)
    Wo = np.asarray(inputs["L0_attn_Wo"], dtype=np.float32)
    bo = np.asarray(inputs["L0_attn_bo"], dtype=np.float32)
    Wf = np.asarray(inputs["L0_ffn_weight"], dtype=np.float32)
    scw = np.asarray(inputs["L0_ffn_scale_weight"], dtype=np.float32)
    fb = np.asarray(inputs["L0_ffn_bias"], dtype=np.float32)

    xT = np.ascontiguousarray(x.T)
    perm = np.concatenate([np.arange(SLOT_HEADS[s][r] * HD,
                                     SLOT_HEADS[s][r] * HD + HD)
                           for r in range(NC) for s in (0, 1)])
    Wo_p = np.ascontiguousarray(Wo[perm, :])
    WfT = np.ascontiguousarray(Wf.T)
    sc_t = np.ascontiguousarray(scw.reshape(NCH, 128).T)
    fb_t = np.ascontiguousarray(fb.reshape(NCH, 128).T)
    bo_t = np.ascontiguousarray(bo.reshape(NCH, 128).T)

    consts = dict(
        ones_b=np.ones((128, 1), dtype=ml_dtypes.bfloat16),
        oneneg_b=np.full((128, 1), -1.0, dtype=ml_dtypes.bfloat16),
        ones_rf=np.ones((1, 128), dtype=np.float32),
        ones_rb=np.ones((1, 128), dtype=ml_dtypes.bfloat16),
        ident_b=np.eye(128, dtype=ml_dtypes.bfloat16),
        ident_f=np.eye(128, dtype=np.float32),
        eps_c=np.full((128, 1), EPS, dtype=np.float32),
        qsc_c=np.concatenate([np.full((128, 1), RSC, np.float32),
                              np.ones((128, 1), np.float32)], axis=1),
    )

    in_maps = []
    for c in range(NC):
        h0, h1 = SLOT_HEADS[0][c], SLOT_HEADS[1][c]
        cols = np.concatenate([np.arange(b + h * HD, b + h * HD + HD)
                               for b, h in ((0, h0), (0, h1), (DM, h0),
                                            (DM, h1), (2 * DM, h0), (2 * DM, h1))])
        Wsl = np.ascontiguousarray(Wqkv[:, cols])
        bsl = np.ascontiguousarray(bqkv[cols].reshape(6, 128).T)
        D = np.stack([_alibi_tiles(SLOPES[h0]), _alibi_tiles(SLOPES[h1])])
        D = np.ascontiguousarray(
            D.reshape(10, KT, QT).transpose(1, 0, 2).reshape(KT, 10 * QT)
            .astype(ml_dtypes.bfloat16))
        in_maps.append(dict(
            xT=xT, xstat=np.ascontiguousarray(xT[:, c * TPC:(c + 1) * TPC]),
            Wsl=Wsl, bsl=bsl, Dt=D, btbl=_bias_table(c),
            Wo_p=Wo_p, WfT=WfT, sc_t=sc_t, fb_t=fb_t, bo_t=bo_t,
            **consts,
        ))
    return in_maps



def build_program(debug=()):
    nc = bass.Bass("TRN2", target_bir_lowering=False)
    din = {}

    def inp(nm, shp, dt=F32):
        din[nm] = nc.dram_tensor(nm, shp, dt, kind="ExternalInput").ap()

    inp("xT", [DM, SEQ]); inp("xstat", [DM, TPC]); inp("Wsl", [DM, 768])
    inp("bsl", [128, 6]); inp("Dt", [128, 10 * QT], BF16); inp("btbl", [128, NITER])
    inp("Wo_p", [DM, DM]); inp("WfT", [DM, DM]); inp("sc_t", [128, NCH])
    inp("fb_t", [128, NCH]); inp("bo_t", [128, NCH])
    inp("ones_b", [128, 1], BF16); inp("oneneg_b", [128, 1], BF16)
    inp("ones_rf", [1, 128]); inp("ones_rb", [1, 128], BF16)
    inp("ident_b", [128, 128], BF16); inp("ident_f", [128, 128])
    inp("eps_c", [128, 1]); inp("qsc_c", [128, 2])
    NCONST = 13

    yT = nc.dram_tensor("yT", [DM, TPC], F32, kind="ExternalOutput").ap()
    dbg = {}
    if "qkv" in debug:
        for nm in ("dbgQ", "dbgK", "dbgV"):
            dbg[nm] = nc.dram_tensor(nm, [256, SEQ], BF16, kind="ExternalOutput").ap()
    if "stats" in debug:
        dbg["dbgST"] = nc.dram_tensor("dbgST", [2, SEQ], BF16, kind="ExternalOutput").ap()
    if "ao" in debug:
        dbg["dbgAO"] = nc.dram_tensor("dbgAO", [2 * DM, QT], BF16, kind="ExternalOutput").ap()

    st_in = nc.dram_tensor("st_in", [2, TPC], F32, kind="Internal").ap()
    st_out = nc.dram_tensor("st_out", [2 * NC, TPC], F32, kind="Internal").ap()
    stats_d = nc.dram_tensor("stats_d", [2, SEQ], BF16, kind="Internal").ap()
    wv_d = nc.dram_tensor("wv_d", [2, 128], BF16, kind="Internal").ap()
    a2a_in = [nc.dram_tensor(f"a2a_in{k}", [DM, QT], BF16, kind="Internal").ap() for k in range(2)]
    a2a_out = [nc.dram_tensor(f"a2a_out{k}", [DM, QT], BF16, kind="Internal").ap() for k in range(2)]

    es = ExitStack()
    with es:
        o_colb = es.enter_context(nc.sbuf_tensor("o_colb", [128, 1], BF16))
        o_negb = es.enter_context(nc.sbuf_tensor("o_negb", [128, 1], BF16))
        o_rowf = es.enter_context(nc.sbuf_tensor("o_rowf", [1, 128], F32))
        o_rowb = es.enter_context(nc.sbuf_tensor("o_rowb", [1, 128], BF16))
        eps_sb = es.enter_context(nc.sbuf_tensor("eps_sb", [128, 1], F32))
        qsc_sb = es.enter_context(nc.sbuf_tensor("qsc_sb", [128, 2], F32))
        bq_sb = es.enter_context(nc.sbuf_tensor("bq_sb", [128, 6], F32))
        sc_sb = es.enter_context(nc.sbuf_tensor("sc_sb", [128, NCH], F32))
        fbt_sb = es.enter_context(nc.sbuf_tensor("fbt_sb", [128, NCH], F32))
        bot_sb = es.enter_context(nc.sbuf_tensor("bot_sb", [128, NCH], F32))
        esAB = ExitStack()
        Qb = [esAB.enter_context(nc.sbuf_tensor(f"Qb{s}", [128, SEQ], BF16)) for s in (0, 1)]
        Kb = [esAB.enter_context(nc.sbuf_tensor(f"Kb{s}", [128, SEQ], BF16)) for s in (0, 1)]
        Vtb = [esAB.enter_context(nc.sbuf_tensor(f"Vt{s}", [128, SEQ], BF16)) for s in (0, 1)]
        btb = esAB.enter_context(nc.sbuf_tensor("btb", [128, NITER], F32))
        lnrT = esAB.enter_context(nc.sbuf_tensor("lnrT", [128, NKP], F32))
        rinv_sb = esAB.enter_context(nc.sbuf_tensor("rinv_sb", [128, NKP], BF16))
        colW = [esAB.enter_context(nc.sbuf_tensor(f"colW{i}", [128, NKP], F32)) for i in range(2)]
        tb_sb = esAB.enter_context(nc.sbuf_tensor("tb_sb", [128, QT], F32))
        esA = ExitStack()
        Wfj = esA.enter_context(nc.sbuf_tensor("Wfj", [128, NCH * 768], BF16))
        xs_sb = [esA.enter_context(nc.sbuf_tensor(f"xs{i}", [128, NCH * QT], BF16)) for i in range(2)]
        bqs_sb = esA.enter_context(nc.sbuf_tensor("bqs_sb", [128, 6], F32))
        w1_sb = esA.enter_context(nc.sbuf_tensor("w1_sb", [128, 6], F32))
        idn = esA.enter_context(nc.sbuf_tensor("idn", [128, 128], BF16))
        idf = esA.enter_context(nc.sbuf_tensor("idf", [128, 128], F32))
        r8 = esA.enter_context(nc.sbuf_tensor("r8", [NC, TPC], F32))
        mu8 = esA.enter_context(nc.sbuf_tensor("mu8", [NC, TPC], F32))
        rT_sb = esA.enter_context(nc.sbuf_tensor("rT_sb", [128, NKP], F32))
        muT_sb = esA.enter_context(nc.sbuf_tensor("muT_sb", [128, NKP], F32))
        rinvf = esA.enter_context(nc.sbuf_tensor("rinvf", [128, NKP], F32))
        w1vnb = [esA.enter_context(nc.sbuf_tensor(f"w1vnb{i}", [128, 128], BF16)) for i in range(2)]
        w1colb = [esA.enter_context(nc.sbuf_tensor(f"w1colb{i}", [128, 1], BF16)) for i in range(2)]
        r8b = esA.enter_context(nc.sbuf_tensor("r8b", [NC, TPC], BF16))
        m8b = esA.enter_context(nc.sbuf_tensor("m8b", [NC, TPC], BF16))
        esA1 = ExitStack()
        stx = [esA1.enter_context(nc.sbuf_tensor(f"stx{i}", [128, TPC], BF16)) for i in range(2)]
        stq = esA1.enter_context(nc.sbuf_tensor("stq", [128, TPC], BF16))
        sums8 = esA1.enter_context(nc.sbuf_tensor("sums8", [NC, TPC], F32))
        sumsq8 = esA1.enter_context(nc.sbuf_tensor("sumsq8", [NC, TPC], F32))
        tmp8 = esA1.enter_context(nc.sbuf_tensor("tmp8", [NC, TPC], F32))
        esA1.close()
        c1_sb = [esA.enter_context(nc.sbuf_tensor(f"c1_{i}", [128, QT], F32)) for i in range(2)]
        ct_sb = [esA.enter_context(nc.sbuf_tensor("ct_0", [128, QT], F32))]
        rBb = [esA.enter_context(nc.sbuf_tensor(f"rBb{i}", [128, QT], BF16)) for i in range(2)]
        mrBb = [esA.enter_context(nc.sbuf_tensor(f"mrBb{i}", [128, QT], BF16)) for i in range(2)]
        esA.enter_context(nc.sbuf_tensor("a2pad", [128, 4096], BF16))
        vstage = [[esA.enter_context(nc.sbuf_tensor(f"vst{v}_{i}", [128, QT], BF16))
                   for i in range(2)] for v in range(2)]
        esA.close()
        esB = ExitStack()
        Dsb = esB.enter_context(nc.sbuf_tensor("Dsb", [128, 10 * QT], BF16))
        E_sb = [esB.enter_context(nc.sbuf_tensor(f"E{i}", [128, QT], BF16)) for i in range(4)]
        P_sb = [esB.enter_context(nc.sbuf_tensor(f"P{i}", [128, QT], BF16)) for i in range(4)]
        rec_sb = esB.enter_context(nc.sbuf_tensor("rec_sb", [1, QT], F32))
        lnt_sb = esB.enter_context(nc.sbuf_tensor("lnt_sb", [1, QT], F32))
        rb_sb = esB.enter_context(nc.sbuf_tensor("rb_sb", [128, QT], F32))
        ao_sb = [esB.enter_context(nc.sbuf_tensor(f"ao{i}", [128, QT], BF16)) for i in range(2)]
        esB.close()
        esAB.close()
        esC = ExitStack()
        woS = [esC.enter_context(nc.sbuf_tensor(f"woS{i}", [128, NCH * 128], BF16)) for i in range(2)]
        Wf2_sb = esC.enter_context(nc.sbuf_tensor("Wf2_sb", [128, NCH * DM], BF16))
        x2_sb = esC.enter_context(nc.sbuf_tensor("x2_sb", [128, NCH * TPC], F32))
        xsl_sb = [esC.enter_context(nc.sbuf_tensor(f"xsl{i}", [128, QT], F32)) for i in range(2)]
        _t2 = esC.enter_context(nc.sbuf_tensor("t2_0", [128, QT], F32))
        t2_sb = [_t2, _t2]
        out_sb = [esC.enter_context(nc.sbuf_tensor(f"out{i}", [128, QT], F32)) for i in range(2)]
        xb2 = [esC.enter_context(nc.sbuf_tensor(f"xb2_{i}", [128, QT], BF16)) for i in range(2)]
        xq2 = [esC.enter_context(nc.sbuf_tensor(f"xq2_{i}", [128, QT], BF16)) for i in range(2)]
        mu2_sb = [esC.enter_context(nc.sbuf_tensor(f"mu2_{k}", [1, QT], F32)) for k in range(2)]
        vr2_sb = [esC.enter_context(nc.sbuf_tensor(f"vr2_{k}", [1, QT], F32)) for k in range(2)]
        r2_sb = [esC.enter_context(nc.sbuf_tensor(f"r2_{k}", [1, QT], F32)) for k in range(2)]
        r2B = [esC.enter_context(nc.sbuf_tensor(f"r2B{k}", [128, QT], F32)) for k in range(2)]
        m2B = [esC.enter_context(nc.sbuf_tensor(f"m2B{k}", [128, QT], F32)) for k in range(2)]
        esC1 = ExitStack()
        am_sb = [esC1.enter_context(nc.sbuf_tensor(f"am{k}", [128, NCH * QT], BF16)) for k in range(2)]
        esC1.close()
        xn_sb = [esC.enter_context(nc.sbuf_tensor(f"xn{k}", [128, NCH * QT], BF16)) for k in range(2)]
        esC.close()

        esP1 = ExitStack()
        pst = [esP1.enter_context(nc.psum_tensor(f"pst{i}", [1, QT], F32)) for i in range(4)]
        esP1.close()
        esPA = ExitStack()
        pw1 = esPA.enter_context(nc.psum_tensor("pw1", [128, 6], F32))
        prT = esPA.enter_context(nc.psum_tensor("prT", [128, 128], F32))
        esPA.enter_context(nc.psum_tensor("pdum0", [1, 4], F32))
        esPA.enter_context(nc.psum_tensor("pdum1", [1, 4], F32))
        pacc = [esPA.enter_context(nc.psum_tensor(f"pacc{i}", [128, QT], F32)) for i in range(2)]
        ptr = [esPA.enter_context(nc.psum_tensor(f"ptr{i}", [128, QT], BF16)) for i in range(2)]
        esPA.close()
        esPB = ExitStack()
        pS = [esPB.enter_context(nc.psum_tensor(f"pS{i}", [128, QT], F32)) for i in range(3)]
        pO = [esPB.enter_context(nc.psum_tensor(f"pO{i}", [128, QT], F32)) for i in range(2)]
        pSum = [esPB.enter_context(nc.psum_tensor(f"pSum{i}", [1, QT], F32)) for i in range(2)]
        pBc = esPB.enter_context(nc.psum_tensor("pBc", [128, QT], F32))
        esPB.close()
        esPC1 = ExitStack()
        pWo = [esPC1.enter_context(nc.psum_tensor(f"pWo{i}", [128, QT], F32)) for i in range(2)]
        pL = [esPC1.enter_context(nc.psum_tensor(f"pL{i}", [1, QT], F32)) for i in range(4)]
        esPC1.close()
        esPC2 = ExitStack()
        pF = [esPC2.enter_context(nc.psum_tensor(f"pF{i}", [128, QT], F32)) for i in range(2)]
        pB2 = [esPC2.enter_context(nc.psum_tensor(f"pB2_{i}", [128, QT], F32)) for i in range(2)]
        esPC2.close()

        sem = {}
        for nm in ("d_sq", "p_st", "d_stv", "s_stin", "ccS", "ccE", "ccO",
                   "s_gout", "d_mu", "s_rx", "d_mr", "s_rr", "s_std",
                   "g_w", "g_d", "p_w1", "d_w1", "d_r8b", "p_end", "g_wob", "g_wfb",
                   "p_bc", "d_bc", "p_fc", "s_fc", "g_ct", "d_corr",
                   "p_tr", "d_tr", "fA",
                   "d_rv", "d_wv", "s_wv", "p_rT", "d_rT", "s_lnr", "d_cw", "d_vb",
                   "p_S", "s_E", "d_P", "p_PV", "d_rec", "p_rb", "d_ao",
                   "fB",
                   "g_am0", "g_am1", "p_wo", "d_x2",
                   "d_xb", "p_ln", "s_mu2", "d_v2", "s_r2", "d_m2", "p_b2",
                   "d_b2", "d_xn", "p_ff", "d_out", "dbg",
                   "chV", "chS", "chG"):
            sem[nm] = es.enter_context(nc.semaphore(nm))
        for nm in ("g_stx", "g_x", "s_ao", "s_xsl", "s_out", "g_wo", "g_wf", "s_slice"):
            sem[nm] = [es.enter_context(nc.semaphore(f"{nm}{i}")) for i in range(2)]

        block = es.enter_context(nc.Block())

        def d_col(s, tau, kp):
            v = 0 if kp < 4 * tau else 1 + (kp - 4 * tau)
            return (s * 5 + v) * QT

        def qk_dst(tt, fc):
            if fc < 2:
                return Qb[fc][:, tt * QT:(tt + 1) * QT]
            return Kb[fc - 2][:, tt * QT:(tt + 1) * QT]

        @block.gpsimd
        def _(g):
            G = _Chain(g, sem["chG"])

            def stx_load(ch):
                if ch >= 2:
                    g.wait_ge(sem["p_st"], ch - 1)
                g.dma_start(out=stx[ch % 2][:],
                            in_=din["xstat"][ch * 128:(ch + 1) * 128, :]
                            ).then_inc(sem["g_stx"][ch % 2], 16)

            def x_load(tt):
                if tt >= 2:
                    g.wait_ge(sem["p_fc"], 6 * (tt - 1))
                g.dma_start(out=xs_sb[tt % 2][:].rearrange("p (n u) -> p n u", n=NCH),
                            in_=din["xT"][:, tt * QT:(tt + 1) * QT]
                                .rearrange("(n p) u -> p n u", p=128)
                            ).then_inc(sem["g_x"][tt % 2], 16)

            stx_load(0); stx_load(1)
            g.dma_start(out=Wfj[:].rearrange("p (n f) -> p n f", n=NCH),
                        in_=din["Wsl"].rearrange("(n p) f -> p n f", p=128)
                        ).then_inc(sem["g_w"], 16)
            x_load(0); x_load(1)
            for ch in range(2, NCH):
                stx_load(ch)
            for tt in range(2, NQT):
                x_load(tt)
                if tt == 4:
                    g.wait_ge(sem["s_stin"], 32)
                    g.collective_compute("AllGather", ALU.bypass,
                                         replica_groups=[list(range(NC))],
                                         ins=[st_in], outs=[st_out]).then_inc(sem["ccS"], 1)
            g.sem_inc(sem["fA"], 1)
            g.wait_ge(sem["p_fc"], 6 * NQT)
            g.dma_start(out=Dsb[:], in_=din["Dt"]).then_inc(sem["g_d"], 16)
            g.wait_ge(sem["fA"], 5)
            g.wait_ge(sem["s_ao"][0], 16 * (NWIN_EVEN // 2))
            g.wait_ge(sem["s_ao"][1], 16 * (NWIN_EVEN // 2))
            g.collective_compute("AllToAll", ALU.bypass,
                                 replica_groups=[list(range(NC))],
                                 ins=[a2a_in[0]], outs=[a2a_out[0]]).then_inc(sem["ccE"], 1)
            g.wait_ge(sem["ccE"], 1)
            g.dma_start(out=am_sb[0][:].rearrange("p (n u) -> p n u", n=NCH),
                        in_=a2a_out[0].rearrange("(n p) u -> p n u", p=128)
                        ).then_inc(sem["g_am0"], 16)
            g.sem_inc(sem["fB"], 1)
            w01 = max(w for w, win in enumerate(WINDOWS)
                      if win["s"] == 0 and win["tau"] <= 7)

            def woS_load(j):
                if j >= 2:
                    g.wait_ge(sem["p_wo"], j - 1)
                g.dma_start(out=woS[j % 2][:].rearrange("p (n f) -> p n f", n=NCH),
                            in_=din["Wo_p"][:, (j % NCH) * 128:(j % NCH + 1) * 128]
                                .rearrange("(n p) f -> p n f", p=128)
                            ).then_inc(sem["g_wo"][j % 2], 16)

            def wf_chunk(c):
                g.dma_start(out=Wf2_sb[:, c * 4 * DM:(c + 1) * 4 * DM]
                            .rearrange("p (n f) -> p n f", n=4),
                            in_=din["WfT"][c * 512:(c + 1) * 512, :]
                            .rearrange("(n p) f -> p n f", p=128)
                            ).then_inc(sem["g_wf"][0], 16)

            g.wait_ge(sem["d_ao"], w01 + 1)
            woS_load(0)
            woS_load(1)
            g.wait_ge(sem["p_end"], 1)
            wf_chunk(0)
            woS_load(2)
            woS_load(3)
            woS_load(4)
            g.wait_ge(sem["s_ao"][0], 16 * (NWIN // 2))
            g.wait_ge(sem["s_ao"][1], 16 * (NWIN // 2))
            g.collective_compute("AllToAll", ALU.bypass,
                                 replica_groups=[list(range(NC))],
                                 ins=[a2a_in[1]], outs=[a2a_out[1]]).then_inc(sem["ccO"], 1)
            for j in range(5, 2 * NCH):
                if j == 17:
                    g.wait_ge(sem["ccO"], 1)
                    g.dma_start(out=am_sb[1][:].rearrange("p (n u) -> p n u", n=NCH),
                                in_=a2a_out[1].rearrange("(n p) u -> p n u", p=128)
                                ).then_inc(sem["g_am1"], 16)
                woS_load(j)
                if j in (12, 19, 26):
                    c = (12, 19, 26).index(j) + 1
                    wf_chunk(c)

        @block.sync
        def _(sy):
            for nm, dst in (("btbl", btb), ("ones_b", o_colb), ("oneneg_b", o_negb),
                            ("ones_rf", o_rowf), ("ones_rb", o_rowb),
                            ("ident_b", idn), ("ident_f", idf), ("bsl", bq_sb),
                            ("sc_t", sc_sb), ("fb_t", fbt_sb), ("bo_t", bot_sb),
                            ("eps_c", eps_sb), ("qsc_c", qsc_sb)):
                sy.dma_start(out=dst[:], in_=din[nm]).then_inc(sem["s_std"], 16)
            sy.wait_ge(sem["d_stv"], 1)
            sy.dma_start(out=st_in[0:1, :], in_=sums8[0:1, :]).then_inc(sem["s_stin"], 16)
            sy.dma_start(out=st_in[1:2, :], in_=sumsq8[0:1, :]).then_inc(sem["s_stin"], 16)
            sy.wait_ge(sem["ccS"], 1)
            sy.dma_start(out=sums8[:].rearrange("p (o t) -> p o t", o=1),
                         in_=st_out.rearrange("(n a) t -> n a t", a=2)[:, 0:1, :]
                         ).then_inc(sem["s_gout"], 16)
            sy.dma_start(out=sumsq8[:].rearrange("p (o t) -> p o t", o=1),
                         in_=st_out.rearrange("(n a) t -> n a t", a=2)[:, 1:2, :]
                         ).then_inc(sem["s_gout"], 16)
            sy.wait_ge(sem["d_r8b"], 1)
            sy.dma_start(out=stats_d[0:1, :].rearrange("a (p t) -> (a p) t", p=NC),
                         in_=r8b[:]).then_inc(sem["s_rr"], 16)
            sy.dma_start(out=stats_d[1:2, :].rearrange("a (p t) -> (a p) t", p=NC),
                         in_=m8b[:]).then_inc(sem["s_rr"], 16)
            for vs in range(2):
                sy.wait_ge(sem["d_wv"], vs + 1)
                sy.dma_start(out=wv_d[vs].rearrange("(k b) -> k b", b=1),
                             in_=w1colb[vs][:]).then_inc(sem["s_wv"], 16)
                sy.wait_ge(sem["s_wv"], 32 * vs + 16)
                sy.dma_start(out=w1vnb[vs][:],
                             in_=wv_d[vs:vs + 1, :].partition_broadcast(128)
                             ).then_inc(sem["s_wv"], 16)
            sy.wait_ge(sem["s_rr"], 32)
            for bt in range(NQT):
                if bt >= 2:
                    sy.wait_ge(sem["d_corr"], 4 * bt - 4)
                sy.dma_start(out=rBb[bt % 2][:],
                             in_=stats_d[0:1, bt * QT:(bt + 1) * QT]
                                 .partition_broadcast(128)
                             ).then_inc(sem["s_slice"][bt % 2], 16)
                sy.dma_start(out=mrBb[bt % 2][:],
                             in_=stats_d[1:2, bt * QT:(bt + 1) * QT]
                                 .partition_broadcast(128)
                             ).then_inc(sem["s_slice"][bt % 2], 16)
            if "stats" in debug:
                sy.wait_ge(sem["s_rr"], 32)
                sy.dma_start(out=dbg["dbgST"], in_=stats_d).then_inc(sem["dbg"], 16)
            if "qkv" in debug:
                sy.wait_ge(sem["d_corr"], 4 * NQT)
                sy.wait_ge(sem["d_vb"], 2 * NKP)
                for s in (0, 1):
                    sy.dma_start(out=dbg["dbgQ"][s * 128:(s + 1) * 128, :],
                                 in_=Qb[s][:]).then_inc(sem["dbg"], 16)
                    sy.dma_start(out=dbg["dbgK"][s * 128:(s + 1) * 128, :],
                                 in_=Kb[s][:]).then_inc(sem["dbg"], 16)
                    sy.dma_start(out=dbg["dbgV"][s * 128:(s + 1) * 128, :],
                                 in_=Vtb[s][:]).then_inc(sem["dbg"], 16)
            sy.sem_inc(sem["fA"], 1)
            for w, win in enumerate(WINDOWS):
                sy.wait_ge(sem["d_ao"], w + 1)
                cons, half = win["tau"] // 2, win["tau"] % 2
                r0 = cons * 256 + win["s"] * 128
                sy.dma_start(out=a2a_in[half][r0:r0 + 128, :],
                             in_=ao_sb[w % 2][:]).then_inc(sem["s_ao"][w % 2], 16)
            if "ao" in debug:
                for half in range(2):
                    sy.dma_start(out=dbg["dbgAO"][half * DM:(half + 1) * DM, :],
                                 in_=a2a_in[half]).then_inc(sem["dbg"], 16)
            sy.sem_inc(sem["fB"], 1)
            for m in range(2 * NCH):
                k, oc = m // NCH, m % NCH
                if m >= 2:
                    sy.wait_ge(sem["d_x2"], m - 1)
                sy.dma_start(out=xsl_sb[m % 2][:],
                             in_=din["xstat"][oc * 128:(oc + 1) * 128,
                                              k * QT:(k + 1) * QT]
                             ).then_inc(sem["s_xsl"][m % 2], 16)
            for k in range(2):
                for oc in range(NCH):
                    m = NCH * k + oc
                    sy.wait_ge(sem["d_out"], m + 1)
                    sy.dma_start(out=yT[oc * 128:(oc + 1) * 128,
                                        k * QT:(k + 1) * QT],
                                 in_=out_sb[m % 2][:]).then_inc(sem["s_out"][m % 2], 16)
            sy.wait_ge(sem["s_out"][0], 16 * NCH)
            sy.wait_ge(sem["s_out"][1], 16 * NCH)

        @block.tensor
        def _(t):
            t.wait_ge(sem["s_std"], 16 * NCONST)

            def stats_ch(ch):
                t.wait_ge(sem["g_stx"][ch % 2], 16 * (ch // 2 + 1))
                t.wait_ge(sem["d_sq"], ch + 1)
                for half in range(2):
                    c0 = half * QT
                    t.matmul(pst[half][:], o_colb[:], stx[ch % 2][:, c0:c0 + QT],
                             start=(ch == 0), stop=(ch == NCH - 1))
                    mm = t.matmul(pst[2 + half][:], o_colb[:], stq[:, c0:c0 + QT],
                                  start=(ch == 0), stop=(ch == NCH - 1))
                    if half == 1:
                        mm.then_inc(sem["p_st"], 1)

            def w1_group():
                t.wait_ge(sem["g_w"], 16)
                t.wait_ge(sem["d_stv"], 1)
                for fc in range(6):
                    for ic in range(NCH):
                        mm = t.matmul(pw1[:, fc:fc + 1],
                                      Wfj[:, ic * 768 + fc * 128:ic * 768 + fc * 128 + 128],
                                      o_negb[:], start=(fc == 0 and ic == 0),
                                      stop=(fc == 5 and ic == NCH - 1))
                        if fc == 5 and ic == NCH - 1:
                            mm.then_inc(sem["p_w1"], 1)

            stats_ch(0)
            stats_ch(1)
            t.wait_ge(sem["g_w"], 16)

            def v_transposes(ti):
                for vs in range(2):
                    t.wait_ge(sem["s_fc"], 6 * ti + 5 + vs)
                    if ti >= 1:
                        t.wait_ge(sem["d_tr"], 2 * (ti - 1) + vs + 1)
                    for sub in range(4):
                        mm = t.transpose(ptr[vs][:, sub * 128:(sub + 1) * 128],
                                         vstage[vs][ti % 2][:, sub * 128:(sub + 1) * 128],
                                         idn[:])
                    mm.then_inc(sem["p_tr"], 1)

            def row_transposes():
                t.wait_ge(sem["d_mr"], 1)
                for gi, src in enumerate((r8, mu8)):
                    for c in range(8):
                        n = 8 * gi + c
                        if n >= 1:
                            t.wait_ge(sem["d_rT"], n)
                        t.transpose(prT[:, 0:8], src[:, 128 * c:128 * (c + 1)],
                                    idf[0:8, 0:8]).then_inc(sem["p_rT"], 1)

            for tt in range(NQT):
                if tt >= 1:
                    v_transposes(tt - 1)
                if tt == 1:
                    stats_ch(14)
                    stats_ch(15)
                    w1_group()
                if tt == BCLAG:
                    row_transposes()
                t.wait_ge(sem["g_x"][tt % 2], 16 * (tt // 2 + 1))
                for fc in range(6):
                    idx = 6 * tt + fc
                    if tt == 0 and fc >= 1:
                        stats_ch(2 * fc)
                        stats_ch(2 * fc + 1)
                    if tt == 0 and fc == 5:
                        stats_ch(12)
                        stats_ch(13)
                    if idx >= 2:
                        t.wait_ge(sem["s_fc"], idx - 1)
                    for ic in range(NCH):
                        mm = t.matmul(pacc[idx % 2][:],
                                      Wfj[:, ic * 768 + fc * 128:ic * 768 + fc * 128 + 128],
                                      xs_sb[tt % 2][:, ic * QT:(ic + 1) * QT],
                                      start=(ic == 0), stop=(ic == NCH - 1))
                        if ic == NCH - 1:
                            mm.then_inc(sem["p_fc"], 1)
            v_transposes(NQT - 1)
            t.sem_inc(sem["fA"], 1)
            t.wait_ge(sem["g_d"], 16)
            vbmap = {}
            _nvb = 0
            V_AT_PE = {6: (0,), 7: (1, 2), 8: (3, 4), 9: (5, 6), 10: (7, 8),
                       11: (9, 10), 12: (11,), 13: (12,), 14: (13,), 15: (14, 15)}
            for _tt in range(NQT):
                for _ti in V_AT_PE.get(_tt, ()):
                    for _vs in range(2):
                        for _kp in range(4 * _ti, 4 * _ti + 4):
                            _nvb += 1
                            vbmap[(_vs, _kp)] = _nvb
            cur_corr = [0]
            cur_vb = [0]

            def gate_corr(need):
                if need > cur_corr[0]:
                    t.wait_ge(sem["d_corr"], need)
                    cur_corr[0] = need

            def gate_vb(need):
                if need > cur_vb[0]:
                    t.wait_ge(sem["d_vb"], need)
                    cur_vb[0] = need
            bc_at = {}
            for w, win in enumerate(WINDOWS):
                bc_at.setdefault(win["it0"] + win["n"] - 1 + APD + 2, []).append(w)
            for ii in range(NITER + APD):
                if ii < NITER:
                    for w in bc_at.get(ii, []):
                        t.wait_ge(sem["d_rec"], w + 1)
                        t.matmul(pBc[:], o_rowf[:], rec_sb[:], start=True, stop=True
                                 ).then_inc(sem["p_rb"], 1)
                    it = ITERS[ii]
                    if ii == 1:
                        t.wait_ge(sem["d_rT"], 16)
                    if ii >= 3:
                        t.wait_ge(sem["s_E"], ii - 2)
                    gate_corr(4 * it["tau"] + it["s"] + 1)
                    gate_corr(4 * (it["kp"] // 4) + 3 + it["s"])
                    mm = t.matmul(pS[ii % 3][:],
                             Kb[it["s"]][:, it["kp"] * KT:(it["kp"] + 1) * KT],
                             Qb[it["s"]][:, it["tau"] * QT:(it["tau"] + 1) * QT],
                             start=True, stop=True)
                    mm.then_inc(sem["p_S"], 1)
                jj = ii - APD
                if 0 <= jj < NITER:
                    it2 = ITERS[jj]
                    if jj == 0:
                        t.wait_ge(sem["d_rv"], 1)
                        t.wait_ge(sem["s_fc"], 6 * NQT)
                        t.wait_ge(sem["d_tr"], 2 * NQT)
                    if it2["first"] and it2["w"] >= 2:
                        t.wait_ge(sem["d_ao"], it2["w"] - 1)
                    gate_vb(vbmap[(it2["s"], it2["kp"])])
                    t.wait_ge(sem["d_P"], jj + 1)
                    ow = it2["w"] % 2
                    t.matmul(pO[ow][:],
                             Vtb[it2["s"]][:, it2["kp"] * KT:(it2["kp"] + 1) * KT],
                             P_sb[jj % 4][:],
                             start=it2["first"], stop=it2["last"])
                    t.matmul(pSum[ow][:], rinv_sb[:, it2["kp"]:it2["kp"] + 1],
                             P_sb[jj % 4][:],
                             start=it2["first"], stop=it2["last"]
                             ).then_inc(sem["p_PV"], 1)
            for w_list in [bc_at.get(s2, []) for s2 in range(NITER, NITER + APD + 3)]:
                for w in w_list:
                    t.wait_ge(sem["d_rec"], w + 1)
                    t.matmul(pBc[:], o_rowf[:], rec_sb[:], start=True, stop=True
                             ).then_inc(sem["p_rb"], 1)
            t.sem_inc(sem["p_end"], 1)
            t.sem_inc(sem["fB"], 1)
            t.wait_ge(sem["s_E"], NITER)
            t.wait_ge(sem["d_ao"], NWIN)
            for k in range(2):
                t.wait_ge(sem["g_am0" if k == 0 else "g_am1"], 16)
                for oc in range(NCH):
                    m = NCH * k + oc
                    t.wait_ge(sem["g_wo"][m % 2], 16 * (m // 2 + 1))
                    if m >= 2:
                        t.wait_ge(sem["d_x2"], m - 1)
                    for ic in range(NCH):
                        mm = t.matmul(pWo[m % 2][:],
                                      woS[m % 2][:, ic * 128:(ic + 1) * 128],
                                      am_sb[k][:, ic * QT:(ic + 1) * QT],
                                      start=(ic == 0), stop=(ic == NCH - 1))
                        if ic == NCH - 1:
                            mm.then_inc(sem["p_wo"], 1)
                    t.wait_ge(sem["d_xb"], m + 1)
                    t.matmul(pL[2 * k][:], o_colb[:], xb2[m % 2][:],
                             start=(oc == 0), stop=(oc == NCH - 1))
                    t.matmul(pL[2 * k + 1][:], o_colb[:], xq2[m % 2][:],
                             start=(oc == 0), stop=(oc == NCH - 1)
                             ).then_inc(sem["p_ln"], 1)
                    if k == 1 and oc == 1:
                        t.wait_ge(sem["d_m2"], 1)
                        t.matmul(pB2[0][:], o_rowf[:], r2_sb[0][:], start=True, stop=True)
                        t.matmul(pB2[1][:], o_rowf[:], mu2_sb[0][:], start=True, stop=True
                                 ).then_inc(sem["p_b2"], 1)
            t.wait_ge(sem["g_wf"][0], 64)
            t.wait_ge(sem["d_x2"], 32)
            for k in range(2):
                for oc in range(NCH):
                    m = NCH * k + oc
                    if k == 0 and oc == 2:
                        t.wait_ge(sem["d_m2"], 2)
                        t.wait_ge(sem["d_b2"], 1)
                        t.matmul(pB2[0][:], o_rowf[:], r2_sb[1][:], start=True, stop=True)
                        t.matmul(pB2[1][:], o_rowf[:], mu2_sb[1][:], start=True, stop=True
                                 ).then_inc(sem["p_b2"], 1)
                    t.wait_ge(sem["d_xn"], 16 * (k + 1))
                    if m >= 2:
                        t.wait_ge(sem["d_out"], m - 1)
                    for ic in range(NCH):
                        mm = t.matmul(pF[m % 2][:],
                                      Wf2_sb[:, ic * DM + oc * 128:ic * DM + oc * 128 + 128],
                                      xn_sb[k][:, ic * QT:(ic + 1) * QT],
                                      start=(ic == 0), stop=(ic == NCH - 1))
                        if ic == NCH - 1:
                            mm.then_inc(sem["p_ff"], 1)

        @block.scalar
        def _(s_):
            S = _Chain(s_, sem["chS"])
            s_.wait_ge(sem["s_std"], 16 * NCONST)
            for tt in range(NQT):
                for fc in range(6):
                    idx = 6 * tt + fc
                    s_.wait_ge(sem["p_fc"], idx + 1)
                    if fc < 4:
                        dst = qk_dst(tt, fc)
                    else:
                        vs = fc - 4
                        if tt >= 2:
                            s_.wait_ge(sem["p_tr"], 2 * (tt - 2) + vs + 1)
                        dst = vstage[vs][tt % 2][:]
                    S(lambda: s_.activation(dst, pacc[idx % 2][:], AF.Copy),
                      inc=(sem["s_fc"], idx + 1, 1))
                if tt >= 1:
                    for vs in range(2):
                        tr = 2 * (tt - 1) + vs
                        s_.wait_ge(sem["p_tr"], tr + 1)
                        S(lambda: s_.activation(Vtb[vs][:, (tt - 1) * QT:tt * QT],
                                                ptr[vs][:], AF.Copy),
                          inc=(sem["d_tr"], tr + 1, 1))
                if tt == 4:
                    s_.wait_ge(sem["d_mu"], 1)
                    S(lambda: s_.activation(sumsq8[:], tmp8[:], AF.Ln,
                                            bias=eps_sb[0:NC, :]))
                    S(lambda: s_.activation(r8[:], sumsq8[:], AF.Exp, scale=-0.5),
                      inc=(sem["s_rx"], 1, 1))
            for vs in range(2):
                tr = 2 * (NQT - 1) + vs
                s_.wait_ge(sem["p_tr"], tr + 1)
                S(lambda: s_.activation(Vtb[vs][:, (NQT - 1) * QT:NQT * QT],
                                        ptr[vs][:], AF.Copy),
                  inc=(sem["d_tr"], tr + 1, 1))
            s_.wait_ge(sem["d_rT"], 16)
            S(lambda: s_.activation(lnrT[:], rT_sb[:], AF.Ln),
              inc=(sem["s_lnr"], 1, 1))
            s_.sem_inc(sem["fA"], 1)
            s_.wait_ge(sem["fA"], 5)
            srec_at = {}
            for w, win in enumerate(WINDOWS):
                srec_at.setdefault(win["it0"] + win["n"] - 1 + APD, []).append(w)
            for i in range(NITER + APD + 1):
                for w in srec_at.get(i, []):
                    s_.wait_ge(sem["p_PV"], WINDOWS[w]["it0"] + WINDOWS[w]["n"])
                    S(lambda: s_.activation(lnt_sb[:], pSum[w % 2][:], AF.Ln))
                    S(lambda: s_.activation(rec_sb[:], lnt_sb[:], AF.Exp, scale=-1.0),
                      inc=(sem["d_rec"], w + 1, 1))
                if i < NITER:
                    it = ITERS[i]
                    if it["first"]:
                        s_.wait_ge(sem["d_cw"], it["w"] + 1)
                    s_.wait_ge(sem["p_S"], i + 1)
                    if i >= 4:
                        s_.wait_ge(sem["d_P"], i - 3)
                    S(lambda: s_.activation(E_sb[i % 4][:], pS[i % 3][:], AF.Exp,
                                            bias=colW[it["w"] % 2][:, it["j"]:it["j"] + 1],
                                            scale=1.0),
                      inc=(sem["s_E"], i + 1, 1))
            s_.sem_inc(sem["fB"], 1)
            s_.wait_ge(sem["fB"], 5)
            for k in range(2):
                s_.wait_ge(sem["p_ln"], NCH * (k + 1))
                S(lambda: s_.mul(mu2_sb[k][:], pL[2 * k][:], 1.0 / DM))
                S(lambda: s_.mul(vr2_sb[k][:], pL[2 * k + 1][:], 1.0 / DM),
                  inc=(sem["s_mu2"], k + 1, 1))
                s_.wait_ge(sem["d_v2"], k + 1)
                S(lambda: s_.activation(vr2_sb[k][:], vr2_sb[k][:], AF.Ln,
                                        bias=eps_sb[0:1, :]))
                S(lambda: s_.activation(r2_sb[k][:], vr2_sb[k][:], AF.Exp, scale=-0.5),
                  inc=(sem["s_r2"], k + 1, 1))

        @block.vector
        def _(v):
            V = _Chain(v, sem["chV"])
            for ch in range(NCH):
                v.wait_ge(sem["g_stx"][ch % 2], 16 * (ch // 2 + 1))
                if ch >= 1:
                    v.wait_ge(sem["p_st"], ch)
                V(lambda: v.tensor_mul(stq[:], stx[ch % 2][:], stx[ch % 2][:]),
                  inc=(sem["d_sq"], ch + 1, 1))
            v.wait_ge(sem["p_st"], NCH)
            V(lambda: v.tensor_copy(sums8[0:1, 0:QT], pst[0][:]))
            V(lambda: v.tensor_copy(sums8[0:1, QT:2 * QT], pst[1][:]))
            V(lambda: v.tensor_copy(sumsq8[0:1, 0:QT], pst[2][:]))
            V(lambda: v.tensor_copy(sumsq8[0:1, QT:2 * QT], pst[3][:]),
              inc=(sem["d_stv"], 1, 1))
            v.wait_ge(sem["p_w1"], 1)
            V(lambda: v.tensor_scalar_mul(w1_sb[:, 0:2], pw1[:, 0:2], RSC))
            V(lambda: v.tensor_copy(w1_sb[:, 2:6], pw1[:, 2:6]))
            V(lambda: v.tensor_scalar_mul(bqs_sb[:, 0:2], bq_sb[:, 0:2], RSC))
            V(lambda: v.tensor_copy(bqs_sb[:, 2:6], bq_sb[:, 2:6]),
              inc=(sem["d_w1"], 1, 1))
            v.wait_ge(sem["s_gout"], 32)
            V(lambda: v.tensor_scalar_mul(mu8[:], sums8[:], 1.0 / DM))
            V(lambda: v.tensor_scalar_mul(tmp8[:], sumsq8[:], 1.0 / DM))
            V(lambda: v.tensor_mul(sums8[:], mu8[:], mu8[:]))
            V(lambda: v.tensor_sub(tmp8[:], tmp8[:], sums8[:]),
              inc=(sem["d_mu"], 1, 1))
            v.wait_ge(sem["s_rx"], 1)
            V(lambda: v.tensor_mul(sums8[:], mu8[:], r8[:]),
              inc=(sem["d_mr"], 1, 1))
            V(lambda: v.tensor_copy(r8b[:], r8[:]))
            V(lambda: v.tensor_copy(m8b[:], sums8[:]),
              inc=(sem["d_r8b"], 1, 1))
            for vs in range(2):
                V(lambda: v.tensor_copy(w1colb[vs][:], w1_sb[:, 4 + vs:5 + vs]),
                  inc=(sem["d_wv"], vs + 1, 1))
            for gi, dstT in enumerate((rT_sb, muT_sb)):
                for c in range(8):
                    n = 8 * gi + c
                    v.wait_ge(sem["p_rT"], n + 1)
                    V(lambda: v.tensor_copy(
                        dstT[:].rearrange("p (a b) -> p a b", b=8)[:, :, c:c + 1],
                        prT[:, 0:8].rearrange("p (a b) -> p a b", b=1)),
                      inc=(sem["d_rT"], n + 1, 1))
            V(lambda: v.reciprocal(rinvf[:], rT_sb[:]))
            V(lambda: v.tensor_copy(rinv_sb[:], rinvf[:]),
              inc=(sem["d_rv"], 1, 1))
            v.wait_ge(sem["s_wv"], 64)
            V_AT = {6: (0,), 7: (1, 2), 8: (3, 4), 9: (5, 6), 10: (7, 8),
                    11: (9, 10), 12: (11,), 13: (12,), 14: (13,), 15: (14, 15)}
            nvb = 0
            for tt in range(NQT):
                v.wait_ge(sem["s_slice"][tt % 2], 32 * (tt // 2 + 1))
                for fc in range(4):
                    q = 4 * tt + fc
                    v.wait_ge(sem["s_fc"], 6 * tt + fc + 1)
                    dst = qk_dst(tt, fc)
                    V(lambda: v.tensor_scalar(ct_sb[0][:], mrBb[tt % 2][:],
                                              w1_sb[:, fc:fc + 1], bqs_sb[:, fc:fc + 1],
                                              ALU.mult, ALU.add))
                    V(lambda: v.tensor_mul(c1_sb[q % 2][:], dst, rBb[tt % 2][:]))
                    V(lambda: v.scalar_tensor_tensor(
                        dst, c1_sb[q % 2][:],
                        qsc_sb[:, (0 if fc < 2 else 1):(1 if fc < 2 else 2)],
                        ct_sb[0][:], ALU.mult, ALU.add),
                      inc=(sem["d_corr"], q + 1, 1))
                for ti in V_AT.get(tt, ()):
                    for vs in range(2):
                        v.wait_ge(sem["d_tr"], 2 * ti + vs + 1)
                        for kp in range(4 * ti, 4 * ti + 4):
                            sl2 = Vtb[vs][:, kp * KT:(kp + 1) * KT]
                            nvb += 1
                            V(lambda: v.scalar_tensor_tensor(
                                sl2, w1vnb[vs][:], muT_sb[:, kp:kp + 1],
                                sl2, ALU.mult, ALU.add),
                              inc=(sem["d_vb"], nvb, 1))
            v.sem_inc(sem["fA"], 1)
            v.wait_ge(sem["fA"], 5)
            v.wait_ge(sem["s_lnr"], 1)
            for w0 in (0, 1):
                win = WINDOWS[w0]
                V(lambda: v.tensor_add(colW[w0][:, 0:win["n"]],
                                       btb[:, win["it0"]:win["it0"] + win["n"]],
                                       lnrT[:, win["kpos"][0]:win["kpos"][0] + win["n"]]),
                  inc=(sem["d_cw"], w0 + 1, 1))
            cw_at, fin_at = {}, {}
            for w, win in enumerate(WINDOWS):
                if w >= 2:
                    cw_at.setdefault(WINDOWS[w - 1]["it0"], []).append(w)
                fin_at.setdefault(win["it0"] + win["n"] - 1 + APD + 2, []).append(w)
            for i in range(NITER + APD + 5):
                for w in cw_at.get(i, []):
                    win = WINDOWS[w]
                    v.wait_ge(sem["s_E"], WINDOWS[w - 1]["it0"])
                    V(lambda: v.tensor_add(colW[w % 2][:, 0:win["n"]],
                                           btb[:, win["it0"]:win["it0"] + win["n"]],
                                           lnrT[:, win["kpos"][0]:win["kpos"][0] + win["n"]]),
                      inc=(sem["d_cw"], w + 1, 1))
                for w in fin_at.get(i, []):
                    v.wait_ge(sem["p_rb"], w + 1)
                    V(lambda: v.tensor_copy(rb_sb[:], pBc[:]))
                    V(lambda: v.tensor_mul(tb_sb[:], pO[w % 2][:], rb_sb[:]))
                    if w >= 2:
                        v.wait_ge(sem["s_ao"][w % 2], 16 * (w // 2))
                    sv = WINDOWS[w]["s"]
                    V(lambda: v.tensor_scalar_add(ao_sb[w % 2][:], tb_sb[:],
                                                  bq_sb[:, 4 + sv:5 + sv]),
                      inc=(sem["d_ao"], w + 1, 1))
                if i < NITER:
                    it = ITERS[i]
                    v.wait_ge(sem["s_E"], i + 1)
                    if i >= 4:
                        v.wait_ge(sem["p_PV"], i - 3)
                    dc = d_col(it["s"], it["tau"], it["kp"])
                    V(lambda: v.tensor_mul(P_sb[i % 4][:], E_sb[i % 4][:],
                                           Dsb[:, dc:dc + QT]),
                      inc=(sem["d_P"], i + 1, 1))
            v.sem_inc(sem["fB"], 1)
            v.wait_ge(sem["fB"], 5)

            def x2xb(k, oc):
                m = NCH * k + oc
                v.wait_ge(sem["p_wo"], m + 1)
                v.wait_ge(sem["s_xsl"][m % 2], 16 * (m // 2 + 1))
                x2v = x2_sb[:, oc * TPC + k * QT:oc * TPC + (k + 1) * QT]
                V(lambda: v.scalar_tensor_tensor(x2v, pWo[m % 2][:],
                                                 bot_sb[:, oc:oc + 1], xsl_sb[m % 2][:],
                                                 ALU.add, ALU.add),
                  inc=(sem["d_x2"], m + 1, 1))
                if m >= 2:
                    v.wait_ge(sem["p_ln"], m - 1)
                V(lambda: v.tensor_copy(xb2[m % 2][:], x2v))
                V(lambda: v.tensor_mul(xq2[m % 2][:], xb2[m % 2][:], xb2[m % 2][:]),
                  inc=(sem["d_xb"], m + 1, 1))

            def ln2_stats(k):
                v.wait_ge(sem["s_mu2"], k + 1)
                V(lambda: v.tensor_mul(r2_sb[k][:], mu2_sb[k][:], mu2_sb[k][:]))
                V(lambda: v.tensor_sub(vr2_sb[k][:], vr2_sb[k][:], r2_sb[k][:]),
                  inc=(sem["d_v2"], k + 1, 1))
                v.wait_ge(sem["s_r2"], k + 1)
                V(lambda: v.tensor_mul(mu2_sb[k][:], mu2_sb[k][:], r2_sb[k][:]),
                  inc=(sem["d_m2"], k + 1, 1))

            def xn_op(k, ic):
                x2v = x2_sb[:, ic * TPC + k * QT:ic * TPC + (k + 1) * QT]
                xnv = xn_sb[k][:, ic * QT:(ic + 1) * QT]
                V(lambda: v.tensor_mul(xnv, x2v, r2B[k][:]))
                V(lambda: v.tensor_sub(xnv, xnv, m2B[k][:]),
                  inc=(sem["d_xn"], 16 * k + ic + 1, 1))

            for oc in range(NCH):
                x2xb(0, oc)
            ln2_stats(0)
            for oc in range(NCH):
                x2xb(1, oc)
                if oc == 2:
                    v.wait_ge(sem["p_b2"], 1)
                    V(lambda: v.tensor_copy(r2B[0][:], pB2[0][:]))
                    V(lambda: v.tensor_copy(m2B[0][:], pB2[1][:]),
                      inc=(sem["d_b2"], 1, 1))
                if 2 <= oc <= 9:
                    xn_op(0, 2 * (oc - 2))
                    xn_op(0, 2 * (oc - 2) + 1)
            ln2_stats(1)
            for k in range(2):
                for oc in range(NCH):
                    m = NCH * k + oc
                    v.wait_ge(sem["p_ff"], m + 1)
                    if m >= 2:
                        v.wait_ge(sem["s_out"][m % 2], 16 * (m // 2))
                    V(lambda: v.tensor_scalar(t2_sb[m % 2][:], pF[m % 2][:],
                                              sc_sb[:, oc:oc + 1], fbt_sb[:, oc:oc + 1],
                                              ALU.mult, ALU.add))
                    V(lambda: v.tensor_add(out_sb[m % 2][:], t2_sb[m % 2][:],
                                           x2_sb[:, oc * TPC + k * QT:oc * TPC + (k + 1) * QT]),
                      inc=(sem["d_out"], m + 1, 1))
                    if k == 0:
                        if oc == 0:
                            v.wait_ge(sem["p_b2"], 2)
                            V(lambda: v.tensor_copy(r2B[1][:], pB2[0][:]))
                            V(lambda: v.tensor_copy(m2B[1][:], pB2[1][:]),
                              inc=(sem["d_b2"], 2, 1))
                        xn_op(1, oc)

    return nc, dbg



def kernel(**inputs):
    in_maps = host_prepare(inputs)
    nc, _ = build_program()
    res = run_bass_kernel_spmd(nc, in_maps, core_ids=list(range(NC)))
    y = np.empty((SEQ, DM), dtype=np.float32)
    for c in range(NC):
        y[c * TPC:(c + 1) * TPC, :] = res.results[c]["yT"].T
    return y.reshape(1, SEQ, DM)


# version 18
# speedup vs baseline: 1.0125x; 1.0125x over previous
import os
os.environ.setdefault("TRN_ALLOW_ICI_ALL_TO_ALL", "1")
import math
from contextlib import ExitStack

import numpy as np
import ml_dtypes

import concourse.bass as bass
import concourse.mybir as mybir
from concourse.bass_utils import run_bass_kernel_spmd

F32 = mybir.dt.float32
BF16 = mybir.dt.bfloat16
AF = mybir.ActivationFunctionType
ALU = mybir.AluOpType


class _Chain:

    def __init__(self, eng, ch_sem):
        self.e, self.ch, self.cnt, self.last = eng, ch_sem, 0, None

    def __call__(self, make, inc=None):
        if self.last is not None:
            self.e.wait_ge(*self.last)
        inst = make()
        if inc is None:
            self.cnt += 1
            inst.then_inc(self.ch, 1)
            self.last = (self.ch, self.cnt)
        else:
            s, absval, n = inc
            inst.then_inc(s, n)
            self.last = (s, absval)
        return inst


SEQ, DM, NH, HD, NC = 8192, 2048, 16, 128, 8
QT, KT = 512, 128
NQT = SEQ // QT
NCH = DM // 128
TPC = SEQ // NC
NKP = SEQ // KT
EPS = 1e-5
RSC = 1.0 / math.sqrt(HD)
CUT = 120.0
W1WIN = 7
NEG = -1.0e30
BCLAG = 5

SLOT_HEADS = [[15 - c for c in range(NC)], list(range(NC))]
SLOPES = [1.0 / 2.0 ** (h / 2.0) for h in range(NH)]
TAU_ORDER = [t for t in range(NQT) if t % 2 == 0] + \
            [t for t in range(NQT) if t % 2 == 1]


def _mk_windows():
    wins, it = [], 0
    for tau in TAU_ORDER:
        for s in (0, 1):
            nf = 4 * tau + 4
            n = nf if s == 0 else min(nf, W1WIN)
            wins.append(dict(s=s, tau=tau, kpos=list(range(nf - n, nf)),
                             it0=it, n=n))
            it += n
    return wins, it


WINDOWS, NITER = _mk_windows()
NWIN = len(WINDOWS)
NWIN_EVEN = NWIN // 2
ITERS = []
for _w, _win in enumerate(WINDOWS):
    for _j, _kp in enumerate(_win["kpos"]):
        ITERS.append(dict(w=_w, j=_j, kp=_kp, s=_win["s"], tau=_win["tau"],
                          first=(_j == 0), last=(_j == _win["n"] - 1)))
APD = 4



def _alibi_tiles(slope):
    dk = np.arange(KT, dtype=np.float64)[:, None]
    dq = np.arange(QT, dtype=np.float64)[None, :]
    t = np.zeros((5, KT, QT), dtype=np.float32)
    t[0] = np.exp(slope * (dk - dq - 128.0)).astype(np.float32)
    for v in range(4):
        arg = slope * (dk + 128.0 * v - dq)
        t[1 + v] = np.where(dk + 128 * v <= dq, np.exp(arg), 0.0).astype(np.float32)
    return t


def _bias_table(c):
    tbl = np.zeros((128, NITER), dtype=np.float32)
    for w in WINDOWS:
        slope = SLOPES[SLOT_HEADS[w["s"]][c]]
        tau = w["tau"]
        for j, kp in enumerate(w["kpos"]):
            if kp >= 4 * tau:
                v = 0.0
            else:
                mind = 512.0 * tau - 128.0 * kp - 128.0
                v = NEG if slope * mind > CUT else slope * (128.0 * kp + 128.0 - 512.0 * tau)
            tbl[:, w["it0"] + j] = v
    return tbl


def host_prepare(inputs):
    x = np.asarray(inputs["L0_attn_x"], dtype=np.float32).reshape(SEQ, DM)
    Wqkv = np.asarray(inputs["L0_attn_Wqkv"], dtype=np.float32)
    bqkv = np.asarray(inputs["L0_attn_bqkv"], dtype=np.float32)
    Wo = np.asarray(inputs["L0_attn_Wo"], dtype=np.float32)
    bo = np.asarray(inputs["L0_attn_bo"], dtype=np.float32)
    Wf = np.asarray(inputs["L0_ffn_weight"], dtype=np.float32)
    scw = np.asarray(inputs["L0_ffn_scale_weight"], dtype=np.float32)
    fb = np.asarray(inputs["L0_ffn_bias"], dtype=np.float32)

    xT = np.ascontiguousarray(x.T)
    perm = np.concatenate([np.arange(SLOT_HEADS[s][r] * HD,
                                     SLOT_HEADS[s][r] * HD + HD)
                           for r in range(NC) for s in (0, 1)])
    Wo_p = np.ascontiguousarray(Wo[perm, :])
    WfT = np.ascontiguousarray(Wf.T)
    sc_t = np.ascontiguousarray(scw.reshape(NCH, 128).T)
    fb_t = np.ascontiguousarray(fb.reshape(NCH, 128).T)
    bo_t = np.ascontiguousarray(bo.reshape(NCH, 128).T)

    consts = dict(
        ones_b=np.ones((128, 1), dtype=ml_dtypes.bfloat16),
        oneneg_b=np.full((128, 1), -1.0, dtype=ml_dtypes.bfloat16),
        ones_rf=np.ones((1, 128), dtype=np.float32),
        ones_rb=np.ones((1, 128), dtype=ml_dtypes.bfloat16),
        ident_b=np.eye(128, dtype=ml_dtypes.bfloat16),
        ident_f=np.eye(128, dtype=np.float32),
        eps_c=np.full((128, 1), EPS, dtype=np.float32),
        qsc_c=np.concatenate([np.full((128, 1), RSC, np.float32),
                              np.ones((128, 1), np.float32)], axis=1),
    )

    in_maps = []
    for c in range(NC):
        h0, h1 = SLOT_HEADS[0][c], SLOT_HEADS[1][c]
        cols = np.concatenate([np.arange(b + h * HD, b + h * HD + HD)
                               for b, h in ((0, h0), (0, h1), (DM, h0),
                                            (DM, h1), (2 * DM, h0), (2 * DM, h1))])
        Wsl = np.ascontiguousarray(Wqkv[:, cols])
        bsl = np.ascontiguousarray(bqkv[cols].reshape(6, 128).T)
        D = np.stack([_alibi_tiles(SLOPES[h0]), _alibi_tiles(SLOPES[h1])])
        D = np.ascontiguousarray(
            D.reshape(10, KT, QT).transpose(1, 0, 2).reshape(KT, 10 * QT)
            .astype(ml_dtypes.bfloat16))
        in_maps.append(dict(
            xT=xT, xstat=np.ascontiguousarray(xT[:, c * TPC:(c + 1) * TPC]),
            Wsl=Wsl, bsl=bsl, Dt=D, btbl=_bias_table(c),
            Wo_p=Wo_p, WfT=WfT, sc_t=sc_t, fb_t=fb_t, bo_t=bo_t,
            **consts,
        ))
    return in_maps



def build_program(debug=()):
    nc = bass.Bass("TRN2", target_bir_lowering=False)
    din = {}

    def inp(nm, shp, dt=F32):
        din[nm] = nc.dram_tensor(nm, shp, dt, kind="ExternalInput").ap()

    inp("xT", [DM, SEQ]); inp("xstat", [DM, TPC]); inp("Wsl", [DM, 768])
    inp("bsl", [128, 6]); inp("Dt", [128, 10 * QT], BF16); inp("btbl", [128, NITER])
    inp("Wo_p", [DM, DM]); inp("WfT", [DM, DM]); inp("sc_t", [128, NCH])
    inp("fb_t", [128, NCH]); inp("bo_t", [128, NCH])
    inp("ones_b", [128, 1], BF16); inp("oneneg_b", [128, 1], BF16)
    inp("ones_rf", [1, 128]); inp("ones_rb", [1, 128], BF16)
    inp("ident_b", [128, 128], BF16); inp("ident_f", [128, 128])
    inp("eps_c", [128, 1]); inp("qsc_c", [128, 2])
    NCONST = 13

    yT = nc.dram_tensor("yT", [DM, TPC], F32, kind="ExternalOutput").ap()
    dbg = {}
    if "qkv" in debug:
        for nm in ("dbgQ", "dbgK", "dbgV"):
            dbg[nm] = nc.dram_tensor(nm, [256, SEQ], BF16, kind="ExternalOutput").ap()
    if "stats" in debug:
        dbg["dbgST"] = nc.dram_tensor("dbgST", [2, SEQ], BF16, kind="ExternalOutput").ap()
    if "ao" in debug:
        dbg["dbgAO"] = nc.dram_tensor("dbgAO", [2 * DM, QT], BF16, kind="ExternalOutput").ap()

    st_in = nc.dram_tensor("st_in", [2, TPC], F32, kind="Internal").ap()
    st_out = nc.dram_tensor("st_out", [2 * NC, TPC], F32, kind="Internal").ap()
    stats_d = nc.dram_tensor("stats_d", [2, SEQ], BF16, kind="Internal").ap()
    wv_d = nc.dram_tensor("wv_d", [2, 128], BF16, kind="Internal").ap()
    a2a_in = [nc.dram_tensor(f"a2a_in{k}", [DM, QT], BF16, kind="Internal").ap() for k in range(2)]
    a2a_out = [nc.dram_tensor(f"a2a_out{k}", [DM, QT], BF16, kind="Internal").ap() for k in range(2)]

    es = ExitStack()
    with es:
        o_colb = es.enter_context(nc.sbuf_tensor("o_colb", [128, 1], BF16))
        o_negb = es.enter_context(nc.sbuf_tensor("o_negb", [128, 1], BF16))
        o_rowf = es.enter_context(nc.sbuf_tensor("o_rowf", [1, 128], F32))
        o_rowb = es.enter_context(nc.sbuf_tensor("o_rowb", [1, 128], BF16))
        eps_sb = es.enter_context(nc.sbuf_tensor("eps_sb", [128, 1], F32))
        qsc_sb = es.enter_context(nc.sbuf_tensor("qsc_sb", [128, 2], F32))
        bq_sb = es.enter_context(nc.sbuf_tensor("bq_sb", [128, 6], F32))
        sc_sb = es.enter_context(nc.sbuf_tensor("sc_sb", [128, NCH], F32))
        fbt_sb = es.enter_context(nc.sbuf_tensor("fbt_sb", [128, NCH], F32))
        bot_sb = es.enter_context(nc.sbuf_tensor("bot_sb", [128, NCH], F32))
        esAB = ExitStack()
        Qb = [esAB.enter_context(nc.sbuf_tensor(f"Qb{s}", [128, SEQ], BF16)) for s in (0, 1)]
        Kb = [esAB.enter_context(nc.sbuf_tensor(f"Kb{s}", [128, SEQ], BF16)) for s in (0, 1)]
        Vtb = [esAB.enter_context(nc.sbuf_tensor(f"Vt{s}", [128, SEQ], BF16)) for s in (0, 1)]
        btb = esAB.enter_context(nc.sbuf_tensor("btb", [128, NITER], F32))
        lnrT = esAB.enter_context(nc.sbuf_tensor("lnrT", [128, NKP], F32))
        rinv_sb = esAB.enter_context(nc.sbuf_tensor("rinv_sb", [128, NKP], BF16))
        colW = [esAB.enter_context(nc.sbuf_tensor(f"colW{i}", [128, NKP], F32)) for i in range(2)]
        tb_sb = esAB.enter_context(nc.sbuf_tensor("tb_sb", [128, QT], F32))
        esA = ExitStack()
        Wfj = esA.enter_context(nc.sbuf_tensor("Wfj", [128, NCH * 768], BF16))
        xs_sb = [esA.enter_context(nc.sbuf_tensor(f"xs{i}", [128, NCH * QT], BF16)) for i in range(2)]
        bqs_sb = esA.enter_context(nc.sbuf_tensor("bqs_sb", [128, 6], F32))
        w1_sb = esA.enter_context(nc.sbuf_tensor("w1_sb", [128, 6], F32))
        idn = esA.enter_context(nc.sbuf_tensor("idn", [128, 128], BF16))
        idf = esA.enter_context(nc.sbuf_tensor("idf", [128, 128], F32))
        r8 = esA.enter_context(nc.sbuf_tensor("r8", [NC, TPC], F32))
        mu8 = esA.enter_context(nc.sbuf_tensor("mu8", [NC, TPC], F32))
        rT_sb = esA.enter_context(nc.sbuf_tensor("rT_sb", [128, NKP], F32))
        muT_sb = esA.enter_context(nc.sbuf_tensor("muT_sb", [128, NKP], F32))
        rinvf = esA.enter_context(nc.sbuf_tensor("rinvf", [128, NKP], F32))
        w1vnb = [esA.enter_context(nc.sbuf_tensor(f"w1vnb{i}", [128, 128], BF16)) for i in range(2)]
        w1colb = [esA.enter_context(nc.sbuf_tensor(f"w1colb{i}", [128, 1], BF16)) for i in range(2)]
        r8b = esA.enter_context(nc.sbuf_tensor("r8b", [NC, TPC], BF16))
        m8b = esA.enter_context(nc.sbuf_tensor("m8b", [NC, TPC], BF16))
        esA1 = ExitStack()
        stx = [esA1.enter_context(nc.sbuf_tensor(f"stx{i}", [128, TPC], BF16)) for i in range(2)]
        stq = esA1.enter_context(nc.sbuf_tensor("stq", [128, TPC], BF16))
        sums8 = esA1.enter_context(nc.sbuf_tensor("sums8", [NC, TPC], F32))
        sumsq8 = esA1.enter_context(nc.sbuf_tensor("sumsq8", [NC, TPC], F32))
        tmp8 = esA1.enter_context(nc.sbuf_tensor("tmp8", [NC, TPC], F32))
        esA1.close()
        c1_sb = [esA.enter_context(nc.sbuf_tensor(f"c1_{i}", [128, QT], F32)) for i in range(2)]
        ct_sb = [esA.enter_context(nc.sbuf_tensor("ct_0", [128, QT], F32))]
        rBb = [esA.enter_context(nc.sbuf_tensor(f"rBb{i}", [128, QT], BF16)) for i in range(2)]
        mrBb = [esA.enter_context(nc.sbuf_tensor(f"mrBb{i}", [128, QT], BF16)) for i in range(2)]
        esA.enter_context(nc.sbuf_tensor("a2pad", [128, 4096], BF16))
        vstage = [[esA.enter_context(nc.sbuf_tensor(f"vst{v}_{i}", [128, QT], BF16))
                   for i in range(2)] for v in range(2)]
        esA.close()
        esB = ExitStack()
        Dsb = esB.enter_context(nc.sbuf_tensor("Dsb", [128, 10 * QT], BF16))
        E_sb = [esB.enter_context(nc.sbuf_tensor(f"E{i}", [128, QT], BF16)) for i in range(4)]
        P_sb = [esB.enter_context(nc.sbuf_tensor(f"P{i}", [128, QT], BF16)) for i in range(4)]
        rec_sb = esB.enter_context(nc.sbuf_tensor("rec_sb", [1, QT], F32))
        lnt_sb = esB.enter_context(nc.sbuf_tensor("lnt_sb", [1, QT], F32))
        rb_sb = esB.enter_context(nc.sbuf_tensor("rb_sb", [128, QT], F32))
        ao_sb = [esB.enter_context(nc.sbuf_tensor(f"ao{i}", [128, QT], BF16)) for i in range(2)]
        esB.close()
        esAB.close()
        esC = ExitStack()
        woS = [esC.enter_context(nc.sbuf_tensor(f"woS{i}", [128, NCH * 128], BF16)) for i in range(2)]
        Wf2_sb = esC.enter_context(nc.sbuf_tensor("Wf2_sb", [128, NCH * DM], BF16))
        x2_sb = esC.enter_context(nc.sbuf_tensor("x2_sb", [128, NCH * TPC], F32))
        xsl_sb = [esC.enter_context(nc.sbuf_tensor(f"xsl{i}", [128, QT], F32)) for i in range(2)]
        _t2 = esC.enter_context(nc.sbuf_tensor("t2_0", [128, QT], F32))
        t2_sb = [_t2, _t2]
        out_sb = [esC.enter_context(nc.sbuf_tensor(f"out{i}", [128, QT], F32)) for i in range(2)]
        xb2 = [esC.enter_context(nc.sbuf_tensor(f"xb2_{i}", [128, QT], BF16)) for i in range(2)]
        xq2 = [esC.enter_context(nc.sbuf_tensor(f"xq2_{i}", [128, QT], BF16)) for i in range(2)]
        mu2_sb = [esC.enter_context(nc.sbuf_tensor(f"mu2_{k}", [1, QT], F32)) for k in range(2)]
        vr2_sb = [esC.enter_context(nc.sbuf_tensor(f"vr2_{k}", [1, QT], F32)) for k in range(2)]
        r2_sb = [esC.enter_context(nc.sbuf_tensor(f"r2_{k}", [1, QT], F32)) for k in range(2)]
        r2B = [esC.enter_context(nc.sbuf_tensor(f"r2B{k}", [128, QT], F32)) for k in range(2)]
        m2B = [esC.enter_context(nc.sbuf_tensor(f"m2B{k}", [128, QT], F32)) for k in range(2)]
        esC1 = ExitStack()
        am_sb = [esC1.enter_context(nc.sbuf_tensor(f"am{k}", [128, NCH * QT], BF16)) for k in range(2)]
        esC1.close()
        xn_sb = [esC.enter_context(nc.sbuf_tensor(f"xn{k}", [128, NCH * QT], BF16)) for k in range(2)]
        esC.close()

        esP1 = ExitStack()
        pst = [esP1.enter_context(nc.psum_tensor(f"pst{i}", [1, QT], F32)) for i in range(4)]
        esP1.close()
        esPA = ExitStack()
        pw1 = esPA.enter_context(nc.psum_tensor("pw1", [128, 6], F32))
        prT = esPA.enter_context(nc.psum_tensor("prT", [128, 128], F32))
        esPA.enter_context(nc.psum_tensor("pdum0", [1, 4], F32))
        esPA.enter_context(nc.psum_tensor("pdum1", [1, 4], F32))
        pacc = [esPA.enter_context(nc.psum_tensor(f"pacc{i}", [128, QT], F32)) for i in range(2)]
        ptr = [esPA.enter_context(nc.psum_tensor(f"ptr{i}", [128, QT], BF16)) for i in range(2)]
        esPA.close()
        esPB = ExitStack()
        pS = [esPB.enter_context(nc.psum_tensor(f"pS{i}", [128, QT], F32)) for i in range(3)]
        pO = [esPB.enter_context(nc.psum_tensor(f"pO{i}", [128, QT], F32)) for i in range(2)]
        pSum = [esPB.enter_context(nc.psum_tensor(f"pSum{i}", [1, QT], F32)) for i in range(2)]
        pBc = esPB.enter_context(nc.psum_tensor("pBc", [128, QT], F32))
        esPB.close()
        esPC1 = ExitStack()
        pWo = [esPC1.enter_context(nc.psum_tensor(f"pWo{i}", [128, QT], F32)) for i in range(2)]
        pL = [esPC1.enter_context(nc.psum_tensor(f"pL{i}", [1, QT], F32)) for i in range(4)]
        esPC1.close()
        esPC2 = ExitStack()
        pF = [esPC2.enter_context(nc.psum_tensor(f"pF{i}", [128, QT], F32)) for i in range(2)]
        pB2 = [esPC2.enter_context(nc.psum_tensor(f"pB2_{i}", [128, QT], F32)) for i in range(2)]
        esPC2.close()

        sem = {}
        for nm in ("d_sq", "p_st", "d_stv", "s_stin", "ccS", "ccE", "ccO",
                   "s_gout", "d_mu", "s_rx", "d_mr", "s_rr", "s_std",
                   "g_w", "g_d", "p_w1", "d_w1", "d_r8b", "p_end", "g_wob", "g_wfb",
                   "p_bc", "d_bc", "p_fc", "s_fc", "g_ct", "d_corr",
                   "p_tr", "d_tr", "fA",
                   "d_rv", "d_wv", "s_wv", "p_rT", "d_rT", "s_lnr", "d_cw", "d_vb",
                   "p_S", "s_E", "d_P", "p_PV", "d_rec", "p_rb", "d_ao",
                   "fB",
                   "g_am0", "g_am1", "p_wo", "d_x2",
                   "d_xb", "p_ln", "s_mu2", "d_v2", "s_r2", "d_m2", "p_b2",
                   "d_b2", "d_xn", "p_ff", "d_out", "dbg",
                   "chV", "chS", "chG"):
            sem[nm] = es.enter_context(nc.semaphore(nm))
        for nm in ("g_stx", "g_x", "s_ao", "s_xsl", "s_out", "g_wo", "g_wf", "s_slice"):
            sem[nm] = [es.enter_context(nc.semaphore(f"{nm}{i}")) for i in range(2)]

        block = es.enter_context(nc.Block())

        def d_col(s, tau, kp):
            v = 0 if kp < 4 * tau else 1 + (kp - 4 * tau)
            return (s * 5 + v) * QT

        def qk_dst(tt, fc):
            if fc < 2:
                return Qb[fc][:, tt * QT:(tt + 1) * QT]
            return Kb[fc - 2][:, tt * QT:(tt + 1) * QT]

        @block.gpsimd
        def _(g):
            G = _Chain(g, sem["chG"])

            def stx_load(ch):
                if ch >= 2:
                    g.wait_ge(sem["p_st"], ch - 1)
                g.dma_start(out=stx[ch % 2][:],
                            in_=din["xstat"][ch * 128:(ch + 1) * 128, :]
                            ).then_inc(sem["g_stx"][ch % 2], 16)

            def x_load(tt):
                if tt >= 2:
                    g.wait_ge(sem["p_fc"], 6 * (tt - 1))
                g.dma_start(out=xs_sb[tt % 2][:].rearrange("p (n u) -> p n u", n=NCH),
                            in_=din["xT"][:, tt * QT:(tt + 1) * QT]
                                .rearrange("(n p) u -> p n u", p=128)
                            ).then_inc(sem["g_x"][tt % 2], 16)

            stx_load(0); stx_load(1)
            g.dma_start(out=Wfj[:].rearrange("p (n f) -> p n f", n=NCH),
                        in_=din["Wsl"].rearrange("(n p) f -> p n f", p=128)
                        ).then_inc(sem["g_w"], 16)
            x_load(0); x_load(1)
            for ch in range(2, NCH):
                stx_load(ch)
            for tt in range(2, NQT):
                x_load(tt)
                if tt == 4:
                    g.wait_ge(sem["s_stin"], 32)
                    g.collective_compute("AllGather", ALU.bypass,
                                         replica_groups=[list(range(NC))],
                                         ins=[st_in], outs=[st_out]).then_inc(sem["ccS"], 1)
            g.sem_inc(sem["fA"], 1)
            g.wait_ge(sem["p_fc"], 6 * NQT)
            g.dma_start(out=Dsb[:], in_=din["Dt"]).then_inc(sem["g_d"], 16)
            g.wait_ge(sem["fA"], 5)
            g.wait_ge(sem["s_ao"][0], 16 * (NWIN_EVEN // 2))
            g.wait_ge(sem["s_ao"][1], 16 * (NWIN_EVEN // 2))
            g.collective_compute("AllToAll", ALU.bypass,
                                 replica_groups=[list(range(NC))],
                                 ins=[a2a_in[0]], outs=[a2a_out[0]]).then_inc(sem["ccE"], 1)
            g.wait_ge(sem["ccE"], 1)
            g.dma_start(out=am_sb[0][:].rearrange("p (n u) -> p n u", n=NCH),
                        in_=a2a_out[0].rearrange("(n p) u -> p n u", p=128)
                        ).then_inc(sem["g_am0"], 16)
            g.sem_inc(sem["fB"], 1)
            w01 = max(w for w, win in enumerate(WINDOWS)
                      if win["s"] == 0 and win["tau"] <= 7)

            def woS_load(j):
                if j >= 2:
                    g.wait_ge(sem["p_wo"], j - 1)
                g.dma_start(out=woS[j % 2][:].rearrange("p (n f) -> p n f", n=NCH),
                            in_=din["Wo_p"][:, (j % NCH) * 128:(j % NCH + 1) * 128]
                                .rearrange("(n p) f -> p n f", p=128)
                            ).then_inc(sem["g_wo"][j % 2], 16)

            def wf_chunk(c):
                g.dma_start(out=Wf2_sb[:, c * 4 * DM:(c + 1) * 4 * DM]
                            .rearrange("p (n f) -> p n f", n=4),
                            in_=din["WfT"][c * 512:(c + 1) * 512, :]
                            .rearrange("(n p) f -> p n f", p=128)
                            ).then_inc(sem["g_wf"][0], 16)

            g.wait_ge(sem["d_ao"], w01 + 1)
            woS_load(0)
            woS_load(1)
            g.wait_ge(sem["p_end"], 1)
            wf_chunk(0)
            woS_load(2)
            woS_load(3)
            woS_load(4)
            g.wait_ge(sem["s_ao"][0], 16 * (NWIN // 2))
            g.wait_ge(sem["s_ao"][1], 16 * (NWIN // 2))
            g.collective_compute("AllToAll", ALU.bypass,
                                 replica_groups=[list(range(NC))],
                                 ins=[a2a_in[1]], outs=[a2a_out[1]]).then_inc(sem["ccO"], 1)
            for j in range(5, 2 * NCH):
                if j == 17:
                    g.wait_ge(sem["ccO"], 1)
                    g.dma_start(out=am_sb[1][:].rearrange("p (n u) -> p n u", n=NCH),
                                in_=a2a_out[1].rearrange("(n p) u -> p n u", p=128)
                                ).then_inc(sem["g_am1"], 16)
                woS_load(j)
                if j in (12, 19, 26):
                    c = (12, 19, 26).index(j) + 1
                    wf_chunk(c)

        @block.sync
        def _(sy):
            for nm, dst in (("btbl", btb), ("ones_b", o_colb), ("oneneg_b", o_negb),
                            ("ones_rf", o_rowf), ("ones_rb", o_rowb),
                            ("ident_b", idn), ("ident_f", idf), ("bsl", bq_sb),
                            ("sc_t", sc_sb), ("fb_t", fbt_sb), ("bo_t", bot_sb),
                            ("eps_c", eps_sb), ("qsc_c", qsc_sb)):
                sy.dma_start(out=dst[:], in_=din[nm]).then_inc(sem["s_std"], 16)
            sy.wait_ge(sem["d_stv"], 1)
            sy.dma_start(out=st_in[0:1, :], in_=sums8[0:1, :]).then_inc(sem["s_stin"], 16)
            sy.dma_start(out=st_in[1:2, :], in_=sumsq8[0:1, :]).then_inc(sem["s_stin"], 16)
            sy.wait_ge(sem["ccS"], 1)
            sy.dma_start(out=sums8[:].rearrange("p (o t) -> p o t", o=1),
                         in_=st_out.rearrange("(n a) t -> n a t", a=2)[:, 0:1, :]
                         ).then_inc(sem["s_gout"], 16)
            sy.dma_start(out=sumsq8[:].rearrange("p (o t) -> p o t", o=1),
                         in_=st_out.rearrange("(n a) t -> n a t", a=2)[:, 1:2, :]
                         ).then_inc(sem["s_gout"], 16)
            sy.wait_ge(sem["d_r8b"], 1)
            sy.dma_start(out=stats_d[0:1, :].rearrange("a (p t) -> (a p) t", p=NC),
                         in_=r8b[:]).then_inc(sem["s_rr"], 16)
            sy.dma_start(out=stats_d[1:2, :].rearrange("a (p t) -> (a p) t", p=NC),
                         in_=m8b[:]).then_inc(sem["s_rr"], 16)
            for vs in range(2):
                sy.wait_ge(sem["d_wv"], vs + 1)
                sy.dma_start(out=wv_d[vs].rearrange("(k b) -> k b", b=1),
                             in_=w1colb[vs][:]).then_inc(sem["s_wv"], 16)
                sy.wait_ge(sem["s_wv"], 32 * vs + 16)
                sy.dma_start(out=w1vnb[vs][:],
                             in_=wv_d[vs:vs + 1, :].partition_broadcast(128)
                             ).then_inc(sem["s_wv"], 16)
            sy.wait_ge(sem["s_rr"], 32)
            for bt in range(NQT):
                if bt >= 2:
                    sy.wait_ge(sem["d_corr"], 4 * bt - 4)
                sy.dma_start(out=rBb[bt % 2][:],
                             in_=stats_d[0:1, bt * QT:(bt + 1) * QT]
                                 .partition_broadcast(128)
                             ).then_inc(sem["s_slice"][bt % 2], 16)
                sy.dma_start(out=mrBb[bt % 2][:],
                             in_=stats_d[1:2, bt * QT:(bt + 1) * QT]
                                 .partition_broadcast(128)
                             ).then_inc(sem["s_slice"][bt % 2], 16)
            if "stats" in debug:
                sy.wait_ge(sem["s_rr"], 32)
                sy.dma_start(out=dbg["dbgST"], in_=stats_d).then_inc(sem["dbg"], 16)
            if "qkv" in debug:
                sy.wait_ge(sem["d_corr"], 4 * NQT)
                sy.wait_ge(sem["d_vb"], 2 * NKP)
                for s in (0, 1):
                    sy.dma_start(out=dbg["dbgQ"][s * 128:(s + 1) * 128, :],
                                 in_=Qb[s][:]).then_inc(sem["dbg"], 16)
                    sy.dma_start(out=dbg["dbgK"][s * 128:(s + 1) * 128, :],
                                 in_=Kb[s][:]).then_inc(sem["dbg"], 16)
                    sy.dma_start(out=dbg["dbgV"][s * 128:(s + 1) * 128, :],
                                 in_=Vtb[s][:]).then_inc(sem["dbg"], 16)
            sy.sem_inc(sem["fA"], 1)
            for w, win in enumerate(WINDOWS):
                sy.wait_ge(sem["d_ao"], w + 1)
                cons, half = win["tau"] // 2, win["tau"] % 2
                r0 = cons * 256 + win["s"] * 128
                sy.dma_start(out=a2a_in[half][r0:r0 + 128, :],
                             in_=ao_sb[w % 2][:]).then_inc(sem["s_ao"][w % 2], 16)
            if "ao" in debug:
                for half in range(2):
                    sy.dma_start(out=dbg["dbgAO"][half * DM:(half + 1) * DM, :],
                                 in_=a2a_in[half]).then_inc(sem["dbg"], 16)
            sy.sem_inc(sem["fB"], 1)
            for m in range(2 * NCH):
                k, oc = m // NCH, m % NCH
                if m >= 2:
                    sy.wait_ge(sem["d_x2"], m - 1)
                sy.dma_start(out=xsl_sb[m % 2][:],
                             in_=din["xstat"][oc * 128:(oc + 1) * 128,
                                              k * QT:(k + 1) * QT]
                             ).then_inc(sem["s_xsl"][m % 2], 16)
            for k in range(2):
                for oc in range(NCH):
                    m = NCH * k + oc
                    sy.wait_ge(sem["d_out"], m + 1)
                    sy.dma_start(out=yT[oc * 128:(oc + 1) * 128,
                                        k * QT:(k + 1) * QT],
                                 in_=out_sb[m % 2][:]).then_inc(sem["s_out"][m % 2], 16)
            sy.wait_ge(sem["s_out"][0], 16 * NCH)
            sy.wait_ge(sem["s_out"][1], 16 * NCH)

        @block.tensor
        def _(t):
            t.wait_ge(sem["s_std"], 16 * NCONST)

            def stats_ch(ch):
                t.wait_ge(sem["g_stx"][ch % 2], 16 * (ch // 2 + 1))
                t.wait_ge(sem["d_sq"], ch + 1)
                for half in range(2):
                    c0 = half * QT
                    t.matmul(pst[half][:], o_colb[:], stx[ch % 2][:, c0:c0 + QT],
                             start=(ch == 0), stop=(ch == NCH - 1))
                    mm = t.matmul(pst[2 + half][:], o_colb[:], stq[:, c0:c0 + QT],
                                  start=(ch == 0), stop=(ch == NCH - 1))
                    if half == 1:
                        mm.then_inc(sem["p_st"], 1)

            def w1_group():
                t.wait_ge(sem["g_w"], 16)
                t.wait_ge(sem["d_stv"], 1)
                for fc in range(6):
                    for ic in range(NCH):
                        mm = t.matmul(pw1[:, fc:fc + 1],
                                      Wfj[:, ic * 768 + fc * 128:ic * 768 + fc * 128 + 128],
                                      o_negb[:], start=(fc == 0 and ic == 0),
                                      stop=(fc == 5 and ic == NCH - 1))
                        if fc == 5 and ic == NCH - 1:
                            mm.then_inc(sem["p_w1"], 1)

            stats_ch(0)
            stats_ch(1)
            t.wait_ge(sem["g_w"], 16)

            def v_transposes(ti):
                for vs in range(2):
                    t.wait_ge(sem["s_fc"], 6 * ti + 5 + vs)
                    if ti >= 1:
                        t.wait_ge(sem["d_tr"], 2 * (ti - 1) + vs + 1)
                    for sub in range(4):
                        mm = t.transpose(ptr[vs][:, sub * 128:(sub + 1) * 128],
                                         vstage[vs][ti % 2][:, sub * 128:(sub + 1) * 128],
                                         idn[:])
                    mm.then_inc(sem["p_tr"], 1)

            def row_transposes():
                t.wait_ge(sem["d_mr"], 1)
                for gi, src in enumerate((r8, mu8)):
                    for c in range(8):
                        n = 8 * gi + c
                        if n >= 1:
                            t.wait_ge(sem["d_rT"], n)
                        t.transpose(prT[:, 0:8], src[:, 128 * c:128 * (c + 1)],
                                    idf[0:8, 0:8]).then_inc(sem["p_rT"], 1)

            for tt in range(NQT):
                if tt >= 1:
                    v_transposes(tt - 1)
                if tt == 1:
                    stats_ch(14)
                    stats_ch(15)
                    w1_group()
                if tt == BCLAG:
                    row_transposes()
                t.wait_ge(sem["g_x"][tt % 2], 16 * (tt // 2 + 1))
                for fc in range(6):
                    idx = 6 * tt + fc
                    if tt == 0 and fc >= 1:
                        stats_ch(2 * fc)
                        stats_ch(2 * fc + 1)
                    if tt == 0 and fc == 5:
                        stats_ch(12)
                        stats_ch(13)
                    if idx >= 2:
                        t.wait_ge(sem["s_fc"], idx - 1)
                    for ic in range(NCH):
                        mm = t.matmul(pacc[idx % 2][:],
                                      Wfj[:, ic * 768 + fc * 128:ic * 768 + fc * 128 + 128],
                                      xs_sb[tt % 2][:, ic * QT:(ic + 1) * QT],
                                      start=(ic == 0), stop=(ic == NCH - 1))
                        if ic == NCH - 1:
                            mm.then_inc(sem["p_fc"], 1)
            v_transposes(NQT - 1)
            t.sem_inc(sem["fA"], 1)
            t.wait_ge(sem["g_d"], 16)
            vbmap = {}
            _nvb = 0
            V_AT_PE = {6: (0,), 7: (1, 2), 8: (3, 4), 9: (5, 6), 10: (7, 8),
                       11: (9, 10), 12: (11,), 13: (12,), 14: (13,), 15: (14, 15)}
            for _tt in range(NQT):
                for _ti in V_AT_PE.get(_tt, ()):
                    for _vs in range(2):
                        for _kp in range(4 * _ti, 4 * _ti + 4):
                            _nvb += 1
                            vbmap[(_vs, _kp)] = _nvb
            cur_corr = [0]
            cur_vb = [0]

            def gate_corr(need):
                if need > cur_corr[0]:
                    t.wait_ge(sem["d_corr"], need)
                    cur_corr[0] = need

            def gate_vb(need):
                if need > cur_vb[0]:
                    t.wait_ge(sem["d_vb"], need)
                    cur_vb[0] = need
            bc_at = {}
            for w, win in enumerate(WINDOWS):
                bc_at.setdefault(win["it0"] + win["n"] - 1 + APD + 2, []).append(w)
            for ii in range(NITER + APD):
                if ii < NITER:
                    for w in bc_at.get(ii, []):
                        t.wait_ge(sem["d_rec"], w + 1)
                        t.matmul(pBc[:], o_rowf[:], rec_sb[:], start=True, stop=True
                                 ).then_inc(sem["p_rb"], 1)
                    it = ITERS[ii]
                    if ii == 1:
                        t.wait_ge(sem["d_rT"], 16)
                    if ii >= 3:
                        t.wait_ge(sem["s_E"], ii - 2)
                    gate_corr(4 * it["tau"] + it["s"] + 1)
                    gate_corr(4 * (it["kp"] // 4) + 3 + it["s"])
                    mm = t.matmul(pS[ii % 3][:],
                             Kb[it["s"]][:, it["kp"] * KT:(it["kp"] + 1) * KT],
                             Qb[it["s"]][:, it["tau"] * QT:(it["tau"] + 1) * QT],
                             start=True, stop=True)
                    mm.then_inc(sem["p_S"], 1)
                jj = ii - APD
                if 0 <= jj < NITER:
                    it2 = ITERS[jj]
                    if jj == 0:
                        t.wait_ge(sem["d_rv"], 1)
                        t.wait_ge(sem["s_fc"], 6 * NQT)
                        t.wait_ge(sem["d_tr"], 2 * NQT)
                    if it2["first"] and it2["w"] >= 2:
                        t.wait_ge(sem["d_ao"], it2["w"] - 1)
                    gate_vb(vbmap[(it2["s"], it2["kp"])])
                    t.wait_ge(sem["d_P"], jj + 1)
                    ow = it2["w"] % 2
                    t.matmul(pO[ow][:],
                             Vtb[it2["s"]][:, it2["kp"] * KT:(it2["kp"] + 1) * KT],
                             P_sb[jj % 4][:],
                             start=it2["first"], stop=it2["last"])
                    t.matmul(pSum[ow][:], rinv_sb[:, it2["kp"]:it2["kp"] + 1],
                             P_sb[jj % 4][:],
                             start=it2["first"], stop=it2["last"]
                             ).then_inc(sem["p_PV"], 1)
            for w_list in [bc_at.get(s2, []) for s2 in range(NITER, NITER + APD + 3)]:
                for w in w_list:
                    t.wait_ge(sem["d_rec"], w + 1)
                    t.matmul(pBc[:], o_rowf[:], rec_sb[:], start=True, stop=True
                             ).then_inc(sem["p_rb"], 1)
            t.sem_inc(sem["p_end"], 1)
            t.sem_inc(sem["fB"], 1)
            t.wait_ge(sem["s_E"], NITER)
            t.wait_ge(sem["d_ao"], NWIN)
            for k in range(2):
                t.wait_ge(sem["g_am0" if k == 0 else "g_am1"], 16)
                for oc in range(NCH):
                    m = NCH * k + oc
                    t.wait_ge(sem["g_wo"][m % 2], 16 * (m // 2 + 1))
                    if m >= 2:
                        t.wait_ge(sem["d_x2"], m - 1)
                    for ic in range(NCH):
                        mm = t.matmul(pWo[m % 2][:],
                                      woS[m % 2][:, ic * 128:(ic + 1) * 128],
                                      am_sb[k][:, ic * QT:(ic + 1) * QT],
                                      start=(ic == 0), stop=(ic == NCH - 1))
                        if ic == NCH - 1:
                            mm.then_inc(sem["p_wo"], 1)
                    t.wait_ge(sem["d_xb"], m + 1)
                    t.matmul(pL[2 * k][:], o_colb[:], xb2[m % 2][:],
                             start=(oc == 0), stop=(oc == NCH - 1))
                    t.matmul(pL[2 * k + 1][:], o_colb[:], xq2[m % 2][:],
                             start=(oc == 0), stop=(oc == NCH - 1)
                             ).then_inc(sem["p_ln"], 1)
                    if k == 1 and oc == 1:
                        t.wait_ge(sem["d_m2"], 1)
                        t.matmul(pB2[0][:], o_rowf[:], r2_sb[0][:], start=True, stop=True)
                        t.matmul(pB2[1][:], o_rowf[:], mu2_sb[0][:], start=True, stop=True
                                 ).then_inc(sem["p_b2"], 1)
            t.wait_ge(sem["g_wf"][0], 64)
            t.wait_ge(sem["d_x2"], 32)
            for k in range(2):
                for oc in range(NCH):
                    m = NCH * k + oc
                    if k == 0 and oc == 2:
                        t.wait_ge(sem["d_m2"], 2)
                        t.wait_ge(sem["d_b2"], 1)
                        t.matmul(pB2[0][:], o_rowf[:], r2_sb[1][:], start=True, stop=True)
                        t.matmul(pB2[1][:], o_rowf[:], mu2_sb[1][:], start=True, stop=True
                                 ).then_inc(sem["p_b2"], 1)
                    t.wait_ge(sem["d_xn"], 16 * (k + 1))
                    if m >= 2:
                        t.wait_ge(sem["d_out"], m - 1)
                    for ic in range(NCH):
                        mm = t.matmul(pF[m % 2][:],
                                      Wf2_sb[:, ic * DM + oc * 128:ic * DM + oc * 128 + 128],
                                      xn_sb[k][:, ic * QT:(ic + 1) * QT],
                                      start=(ic == 0), stop=(ic == NCH - 1))
                        if ic == NCH - 1:
                            mm.then_inc(sem["p_ff"], 1)

        @block.scalar
        def _(s_):
            S = _Chain(s_, sem["chS"])
            s_.wait_ge(sem["s_std"], 16 * NCONST)
            for tt in range(NQT):
                for fc in range(6):
                    idx = 6 * tt + fc
                    s_.wait_ge(sem["p_fc"], idx + 1)
                    if fc < 4:
                        dst = qk_dst(tt, fc)
                    else:
                        vs = fc - 4
                        if tt >= 2:
                            s_.wait_ge(sem["p_tr"], 2 * (tt - 2) + vs + 1)
                        dst = vstage[vs][tt % 2][:]
                    S(lambda: s_.activation(dst, pacc[idx % 2][:], AF.Copy),
                      inc=(sem["s_fc"], idx + 1, 1))
                if tt >= 1:
                    for vs in range(2):
                        tr = 2 * (tt - 1) + vs
                        s_.wait_ge(sem["p_tr"], tr + 1)
                        S(lambda: s_.activation(Vtb[vs][:, (tt - 1) * QT:tt * QT],
                                                ptr[vs][:], AF.Copy),
                          inc=(sem["d_tr"], tr + 1, 1))
                if tt == 4:
                    s_.wait_ge(sem["d_mu"], 1)
                    S(lambda: s_.activation(sumsq8[:], tmp8[:], AF.Ln,
                                            bias=eps_sb[0:NC, :]))
                    S(lambda: s_.activation(r8[:], sumsq8[:], AF.Exp, scale=-0.5),
                      inc=(sem["s_rx"], 1, 1))
            for vs in range(2):
                tr = 2 * (NQT - 1) + vs
                s_.wait_ge(sem["p_tr"], tr + 1)
                S(lambda: s_.activation(Vtb[vs][:, (NQT - 1) * QT:NQT * QT],
                                        ptr[vs][:], AF.Copy),
                  inc=(sem["d_tr"], tr + 1, 1))
            s_.wait_ge(sem["d_rT"], 16)
            S(lambda: s_.activation(lnrT[:], rT_sb[:], AF.Ln),
              inc=(sem["s_lnr"], 1, 1))
            s_.sem_inc(sem["fA"], 1)
            s_.wait_ge(sem["fA"], 5)
            srec_at = {}
            for w, win in enumerate(WINDOWS):
                srec_at.setdefault(win["it0"] + win["n"] - 1 + APD, []).append(w)
            for i in range(NITER + APD + 1):
                for w in srec_at.get(i, []):
                    s_.wait_ge(sem["p_PV"], WINDOWS[w]["it0"] + WINDOWS[w]["n"])
                    S(lambda: s_.activation(lnt_sb[:], pSum[w % 2][:], AF.Ln))
                    S(lambda: s_.activation(rec_sb[:], lnt_sb[:], AF.Exp, scale=-1.0),
                      inc=(sem["d_rec"], w + 1, 1))
                if i < NITER:
                    it = ITERS[i]
                    if it["first"]:
                        s_.wait_ge(sem["d_cw"], it["w"] + 1)
                    s_.wait_ge(sem["p_S"], i + 1)
                    if i >= 4:
                        s_.wait_ge(sem["d_P"], i - 3)
                    S(lambda: s_.activation(E_sb[i % 4][:], pS[i % 3][:], AF.Exp,
                                            bias=colW[it["w"] % 2][:, it["j"]:it["j"] + 1],
                                            scale=1.0),
                      inc=(sem["s_E"], i + 1, 1))
            s_.sem_inc(sem["fB"], 1)
            s_.wait_ge(sem["fB"], 5)
            for k in range(2):
                s_.wait_ge(sem["p_ln"], NCH * (k + 1))
                S(lambda: s_.mul(mu2_sb[k][:], pL[2 * k][:], 1.0 / DM))
                S(lambda: s_.mul(vr2_sb[k][:], pL[2 * k + 1][:], 1.0 / DM),
                  inc=(sem["s_mu2"], k + 1, 1))
                s_.wait_ge(sem["d_v2"], k + 1)
                S(lambda: s_.activation(vr2_sb[k][:], vr2_sb[k][:], AF.Ln,
                                        bias=eps_sb[0:1, :]))
                S(lambda: s_.activation(r2_sb[k][:], vr2_sb[k][:], AF.Exp, scale=-0.5),
                  inc=(sem["s_r2"], k + 1, 1))

        @block.vector
        def _(v):
            V = _Chain(v, sem["chV"])
            for ch in range(NCH):
                v.wait_ge(sem["g_stx"][ch % 2], 16 * (ch // 2 + 1))
                if ch >= 1:
                    v.wait_ge(sem["p_st"], ch)
                V(lambda: v.tensor_mul(stq[:], stx[ch % 2][:], stx[ch % 2][:]),
                  inc=(sem["d_sq"], ch + 1, 1))
            v.wait_ge(sem["p_st"], NCH)
            V(lambda: v.tensor_copy(sums8[0:1, 0:QT], pst[0][:]))
            V(lambda: v.tensor_copy(sums8[0:1, QT:2 * QT], pst[1][:]))
            V(lambda: v.tensor_copy(sumsq8[0:1, 0:QT], pst[2][:]))
            V(lambda: v.tensor_copy(sumsq8[0:1, QT:2 * QT], pst[3][:]),
              inc=(sem["d_stv"], 1, 1))
            v.wait_ge(sem["p_w1"], 1)
            V(lambda: v.tensor_scalar_mul(w1_sb[:, 0:2], pw1[:, 0:2], RSC))
            V(lambda: v.tensor_copy(w1_sb[:, 2:6], pw1[:, 2:6]))
            V(lambda: v.tensor_scalar_mul(bqs_sb[:, 0:2], bq_sb[:, 0:2], RSC))
            V(lambda: v.tensor_copy(bqs_sb[:, 2:6], bq_sb[:, 2:6]),
              inc=(sem["d_w1"], 1, 1))
            v.wait_ge(sem["s_gout"], 32)
            V(lambda: v.tensor_scalar_mul(mu8[:], sums8[:], 1.0 / DM))
            V(lambda: v.tensor_scalar_mul(tmp8[:], sumsq8[:], 1.0 / DM))
            V(lambda: v.tensor_mul(sums8[:], mu8[:], mu8[:]))
            V(lambda: v.tensor_sub(tmp8[:], tmp8[:], sums8[:]),
              inc=(sem["d_mu"], 1, 1))
            v.wait_ge(sem["s_rx"], 1)
            V(lambda: v.tensor_mul(sums8[:], mu8[:], r8[:]),
              inc=(sem["d_mr"], 1, 1))
            V(lambda: v.tensor_copy(r8b[:], r8[:]))
            V(lambda: v.tensor_copy(m8b[:], sums8[:]),
              inc=(sem["d_r8b"], 1, 1))
            for vs in range(2):
                V(lambda: v.tensor_copy(w1colb[vs][:], w1_sb[:, 4 + vs:5 + vs]),
                  inc=(sem["d_wv"], vs + 1, 1))
            for gi, dstT in enumerate((rT_sb, muT_sb)):
                for c in range(8):
                    n = 8 * gi + c
                    v.wait_ge(sem["p_rT"], n + 1)
                    V(lambda: v.tensor_copy(
                        dstT[:].rearrange("p (a b) -> p a b", b=8)[:, :, c:c + 1],
                        prT[:, 0:8].rearrange("p (a b) -> p a b", b=1)),
                      inc=(sem["d_rT"], n + 1, 1))
            V(lambda: v.reciprocal(rinvf[:], rT_sb[:]))
            V(lambda: v.tensor_copy(rinv_sb[:], rinvf[:]),
              inc=(sem["d_rv"], 1, 1))
            v.wait_ge(sem["s_wv"], 64)
            V_AT = {6: (0,), 7: (1, 2), 8: (3, 4), 9: (5, 6), 10: (7, 8),
                    11: (9, 10), 12: (11,), 13: (12,), 14: (13,), 15: (14, 15)}
            nvb = 0
            for tt in range(NQT):
                v.wait_ge(sem["s_slice"][tt % 2], 32 * (tt // 2 + 1))
                for fc in range(4):
                    q = 4 * tt + fc
                    v.wait_ge(sem["s_fc"], 6 * tt + fc + 1)
                    dst = qk_dst(tt, fc)
                    V(lambda: v.tensor_scalar(ct_sb[0][:], mrBb[tt % 2][:],
                                              w1_sb[:, fc:fc + 1], bqs_sb[:, fc:fc + 1],
                                              ALU.mult, ALU.add))
                    V(lambda: v.tensor_mul(c1_sb[q % 2][:], dst, rBb[tt % 2][:]))
                    V(lambda: v.scalar_tensor_tensor(
                        dst, c1_sb[q % 2][:],
                        qsc_sb[:, (0 if fc < 2 else 1):(1 if fc < 2 else 2)],
                        ct_sb[0][:], ALU.mult, ALU.add),
                      inc=(sem["d_corr"], q + 1, 1))
                for ti in V_AT.get(tt, ()):
                    for vs in range(2):
                        v.wait_ge(sem["d_tr"], 2 * ti + vs + 1)
                        for kp in range(4 * ti, 4 * ti + 4):
                            sl2 = Vtb[vs][:, kp * KT:(kp + 1) * KT]
                            nvb += 1
                            V(lambda: v.scalar_tensor_tensor(
                                sl2, w1vnb[vs][:], muT_sb[:, kp:kp + 1],
                                sl2, ALU.mult, ALU.add),
                              inc=(sem["d_vb"], nvb, 1))
            v.sem_inc(sem["fA"], 1)
            v.wait_ge(sem["fA"], 5)
            v.wait_ge(sem["s_lnr"], 1)
            for w0 in (0, 1):
                win = WINDOWS[w0]
                V(lambda: v.tensor_add(colW[w0][:, 0:win["n"]],
                                       btb[:, win["it0"]:win["it0"] + win["n"]],
                                       lnrT[:, win["kpos"][0]:win["kpos"][0] + win["n"]]),
                  inc=(sem["d_cw"], w0 + 1, 1))
            cw_at, fin_at = {}, {}
            for w, win in enumerate(WINDOWS):
                if w >= 2:
                    cw_at.setdefault(WINDOWS[w - 1]["it0"], []).append(w)
                fin_at.setdefault(win["it0"] + win["n"] - 1 + APD + 2, []).append(w)
            for i in range(NITER + APD + 5):
                for w in cw_at.get(i, []):
                    win = WINDOWS[w]
                    v.wait_ge(sem["s_E"], WINDOWS[w - 1]["it0"])
                    V(lambda: v.tensor_add(colW[w % 2][:, 0:win["n"]],
                                           btb[:, win["it0"]:win["it0"] + win["n"]],
                                           lnrT[:, win["kpos"][0]:win["kpos"][0] + win["n"]]),
                      inc=(sem["d_cw"], w + 1, 1))
                for w in fin_at.get(i, []):
                    v.wait_ge(sem["p_rb"], w + 1)
                    V(lambda: v.tensor_copy(rb_sb[:], pBc[:]))
                    V(lambda: v.tensor_mul(tb_sb[:], pO[w % 2][:], rb_sb[:]))
                    if w >= 2:
                        v.wait_ge(sem["s_ao"][w % 2], 16 * (w // 2))
                    sv = WINDOWS[w]["s"]
                    V(lambda: v.tensor_scalar_add(ao_sb[w % 2][:], tb_sb[:],
                                                  bq_sb[:, 4 + sv:5 + sv]),
                      inc=(sem["d_ao"], w + 1, 1))
                if i < NITER:
                    it = ITERS[i]
                    v.wait_ge(sem["s_E"], i + 1)
                    if i >= 4:
                        v.wait_ge(sem["p_PV"], i - 3)
                    dc = d_col(it["s"], it["tau"], it["kp"])
                    V(lambda: v.tensor_mul(P_sb[i % 4][:], E_sb[i % 4][:],
                                           Dsb[:, dc:dc + QT]),
                      inc=(sem["d_P"], i + 1, 1))
            v.sem_inc(sem["fB"], 1)
            v.wait_ge(sem["fB"], 5)

            def x2xb(k, oc):
                m = NCH * k + oc
                v.wait_ge(sem["p_wo"], m + 1)
                v.wait_ge(sem["s_xsl"][m % 2], 16 * (m // 2 + 1))
                x2v = x2_sb[:, oc * TPC + k * QT:oc * TPC + (k + 1) * QT]
                V(lambda: v.scalar_tensor_tensor(x2v, pWo[m % 2][:],
                                                 bot_sb[:, oc:oc + 1], xsl_sb[m % 2][:],
                                                 ALU.add, ALU.add),
                  inc=(sem["d_x2"], m + 1, 1))
                if m >= 2:
                    v.wait_ge(sem["p_ln"], m - 1)
                V(lambda: v.tensor_copy(xb2[m % 2][:], x2v))
                V(lambda: v.tensor_mul(xq2[m % 2][:], xb2[m % 2][:], xb2[m % 2][:]),
                  inc=(sem["d_xb"], m + 1, 1))

            def ln2_stats(k):
                v.wait_ge(sem["s_mu2"], k + 1)
                V(lambda: v.tensor_mul(r2_sb[k][:], mu2_sb[k][:], mu2_sb[k][:]))
                V(lambda: v.tensor_sub(vr2_sb[k][:], vr2_sb[k][:], r2_sb[k][:]),
                  inc=(sem["d_v2"], k + 1, 1))
                v.wait_ge(sem["s_r2"], k + 1)
                V(lambda: v.tensor_mul(mu2_sb[k][:], mu2_sb[k][:], r2_sb[k][:]),
                  inc=(sem["d_m2"], k + 1, 1))

            def xn_op(k, ic):
                x2v = x2_sb[:, ic * TPC + k * QT:ic * TPC + (k + 1) * QT]
                xnv = xn_sb[k][:, ic * QT:(ic + 1) * QT]
                V(lambda: v.tensor_mul(xnv, x2v, r2B[k][:]))
                V(lambda: v.tensor_sub(xnv, xnv, m2B[k][:]),
                  inc=(sem["d_xn"], 16 * k + ic + 1, 1))

            for oc in range(NCH):
                x2xb(0, oc)
            ln2_stats(0)
            for oc in range(NCH):
                x2xb(1, oc)
                if oc == 2:
                    v.wait_ge(sem["p_b2"], 1)
                    V(lambda: v.tensor_copy(r2B[0][:], pB2[0][:]))
                    V(lambda: v.tensor_copy(m2B[0][:], pB2[1][:]),
                      inc=(sem["d_b2"], 1, 1))
                if 2 <= oc <= 9:
                    xn_op(0, 2 * (oc - 2))
                    xn_op(0, 2 * (oc - 2) + 1)
            ln2_stats(1)
            for k in range(2):
                for oc in range(NCH):
                    m = NCH * k + oc
                    v.wait_ge(sem["p_ff"], m + 1)
                    if m >= 2:
                        v.wait_ge(sem["s_out"][m % 2], 16 * (m // 2))
                    V(lambda: v.tensor_scalar(t2_sb[m % 2][:], pF[m % 2][:],
                                              sc_sb[:, oc:oc + 1], fbt_sb[:, oc:oc + 1],
                                              ALU.mult, ALU.add))
                    V(lambda: v.tensor_add(out_sb[m % 2][:], t2_sb[m % 2][:],
                                           x2_sb[:, oc * TPC + k * QT:oc * TPC + (k + 1) * QT]),
                      inc=(sem["d_out"], m + 1, 1))
                    if k == 0:
                        if oc == 0:
                            v.wait_ge(sem["p_b2"], 2)
                            V(lambda: v.tensor_copy(r2B[1][:], pB2[0][:]))
                            V(lambda: v.tensor_copy(m2B[1][:], pB2[1][:]),
                              inc=(sem["d_b2"], 2, 1))
                        xn_op(1, oc)

    return nc, dbg



def kernel(**inputs):
    in_maps = host_prepare(inputs)
    nc, _ = build_program()
    res = run_bass_kernel_spmd(nc, in_maps, core_ids=list(range(NC)))
    y = np.empty((SEQ, DM), dtype=np.float32)
    for c in range(NC):
        y[c * TPC:(c + 1) * TPC, :] = res.results[c]["yT"].T
    return y.reshape(1, SEQ, DM)


# version 19 (final)
# speedup vs baseline: 1.0666x; 1.0534x over previous
import os
os.environ.setdefault("TRN_ALLOW_ICI_ALL_TO_ALL", "1")
import math
from contextlib import ExitStack

import numpy as np
import ml_dtypes

import concourse.bass as bass
import concourse.mybir as mybir
from concourse.bass_utils import run_bass_kernel_spmd

F32 = mybir.dt.float32
BF16 = mybir.dt.bfloat16
AF = mybir.ActivationFunctionType
ALU = mybir.AluOpType


class _Chain:

    def __init__(self, eng, ch_sem):
        self.e, self.ch, self.cnt, self.last = eng, ch_sem, 0, None

    def __call__(self, make, inc=None):
        if self.last is not None:
            self.e.wait_ge(*self.last)
        inst = make()
        if inc is None:
            self.cnt += 1
            inst.then_inc(self.ch, 1)
            self.last = (self.ch, self.cnt)
        else:
            s, absval, n = inc
            inst.then_inc(s, n)
            self.last = (s, absval)
        return inst


SEQ, DM, NH, HD, NC = 8192, 2048, 16, 128, 8
QT, KT = 512, 128
NQT = SEQ // QT
NCH = DM // 128
TPC = SEQ // NC
NKP = SEQ // KT
EPS = 1e-5
RSC = 1.0 / math.sqrt(HD)
CUT = 120.0
W1WIN = 7
NEG = -1.0e30
BCLAG = 5

SLOT_HEADS = [[15 - c for c in range(NC)], list(range(NC))]
SLOPES = [1.0 / 2.0 ** (h / 2.0) for h in range(NH)]
TAU_ORDER = [t for t in range(NQT) if t % 2 == 0] + \
            [t for t in range(NQT) if t % 2 == 1]


def _mk_windows():
    wins, it = [], 0
    for tau in TAU_ORDER:
        for s in (0, 1):
            nf = 4 * tau + 4
            n = nf if s == 0 else min(nf, W1WIN)
            wins.append(dict(s=s, tau=tau, kpos=list(range(nf - n, nf)),
                             it0=it, n=n))
            it += n
    return wins, it


WINDOWS, NITER = _mk_windows()
NWIN = len(WINDOWS)
NWIN_EVEN = NWIN // 2
ITERS = []
for _w, _win in enumerate(WINDOWS):
    for _j, _kp in enumerate(_win["kpos"]):
        ITERS.append(dict(w=_w, j=_j, kp=_kp, s=_win["s"], tau=_win["tau"],
                          first=(_j == 0), last=(_j == _win["n"] - 1)))
APD = 4



def _alibi_tiles(slope):
    dk = np.arange(KT, dtype=np.float64)[:, None]
    dq = np.arange(QT, dtype=np.float64)[None, :]
    t = np.zeros((5, KT, QT), dtype=np.float32)
    t[0] = np.exp(slope * (dk - dq - 128.0)).astype(np.float32)
    for v in range(4):
        arg = slope * (dk + 128.0 * v - dq)
        t[1 + v] = np.where(dk + 128 * v <= dq, np.exp(arg), 0.0).astype(np.float32)
    return t


def _bias_table(c):
    tbl = np.zeros((128, NITER), dtype=np.float32)
    for w in WINDOWS:
        slope = SLOPES[SLOT_HEADS[w["s"]][c]]
        tau = w["tau"]
        for j, kp in enumerate(w["kpos"]):
            if kp >= 4 * tau:
                v = 0.0
            else:
                mind = 512.0 * tau - 128.0 * kp - 128.0
                v = NEG if slope * mind > CUT else slope * (128.0 * kp + 128.0 - 512.0 * tau)
            tbl[:, w["it0"] + j] = v
    return tbl


def host_prepare(inputs):
    x = np.asarray(inputs["L0_attn_x"], dtype=np.float32).reshape(SEQ, DM)
    Wqkv = np.asarray(inputs["L0_attn_Wqkv"], dtype=np.float32)
    bqkv = np.asarray(inputs["L0_attn_bqkv"], dtype=np.float32)
    Wo = np.asarray(inputs["L0_attn_Wo"], dtype=np.float32)
    bo = np.asarray(inputs["L0_attn_bo"], dtype=np.float32)
    Wf = np.asarray(inputs["L0_ffn_weight"], dtype=np.float32)
    scw = np.asarray(inputs["L0_ffn_scale_weight"], dtype=np.float32)
    fb = np.asarray(inputs["L0_ffn_bias"], dtype=np.float32)

    xT = np.ascontiguousarray(x.T)
    perm = np.concatenate([np.arange(SLOT_HEADS[s][r] * HD,
                                     SLOT_HEADS[s][r] * HD + HD)
                           for r in range(NC) for s in (0, 1)])
    Wo_p = np.ascontiguousarray(Wo[perm, :])
    WfT = np.ascontiguousarray(Wf.T)
    sc_t = np.ascontiguousarray(scw.reshape(NCH, 128).T)
    fb_t = np.ascontiguousarray(fb.reshape(NCH, 128).T)
    bo_t = np.ascontiguousarray(bo.reshape(NCH, 128).T)

    consts = dict(
        ones_b=np.ones((128, 1), dtype=ml_dtypes.bfloat16),
        oneneg_b=np.full((128, 1), -1.0, dtype=ml_dtypes.bfloat16),
        ones_rf=np.ones((1, 128), dtype=np.float32),
        ones_rb=np.ones((1, 128), dtype=ml_dtypes.bfloat16),
        ident_b=np.eye(128, dtype=ml_dtypes.bfloat16),
        ident_f=np.eye(128, dtype=np.float32),
        eps_c=np.full((128, 1), EPS, dtype=np.float32),
        qsc_c=np.concatenate([np.full((128, 1), RSC, np.float32),
                              np.ones((128, 1), np.float32)], axis=1),
    )

    in_maps = []
    for c in range(NC):
        h0, h1 = SLOT_HEADS[0][c], SLOT_HEADS[1][c]
        cols = np.concatenate([np.arange(b + h * HD, b + h * HD + HD)
                               for b, h in ((0, h0), (0, h1), (DM, h0),
                                            (DM, h1), (2 * DM, h0), (2 * DM, h1))])
        Wsl = np.ascontiguousarray(Wqkv[:, cols])
        bsl = np.ascontiguousarray(bqkv[cols].reshape(6, 128).T)
        D = np.stack([_alibi_tiles(SLOPES[h0]), _alibi_tiles(SLOPES[h1])])
        D = np.ascontiguousarray(
            D.reshape(10, KT, QT).transpose(1, 0, 2).reshape(KT, 10 * QT)
            .astype(ml_dtypes.bfloat16))
        in_maps.append(dict(
            xT=xT, xstat=np.ascontiguousarray(xT[:, c * TPC:(c + 1) * TPC]),
            Wsl=Wsl, bsl=bsl, Dt=D, btbl=_bias_table(c),
            Wo_p=Wo_p, WfT=WfT, sc_t=sc_t, fb_t=fb_t, bo_t=bo_t,
            **consts,
        ))
    return in_maps



def build_program(debug=()):
    nc = bass.Bass("TRN2", target_bir_lowering=False)
    din = {}

    def inp(nm, shp, dt=F32):
        din[nm] = nc.dram_tensor(nm, shp, dt, kind="ExternalInput").ap()

    inp("xT", [DM, SEQ]); inp("xstat", [DM, TPC]); inp("Wsl", [DM, 768])
    inp("bsl", [128, 6]); inp("Dt", [128, 10 * QT], BF16); inp("btbl", [128, NITER])
    inp("Wo_p", [DM, DM]); inp("WfT", [DM, DM]); inp("sc_t", [128, NCH])
    inp("fb_t", [128, NCH]); inp("bo_t", [128, NCH])
    inp("ones_b", [128, 1], BF16); inp("oneneg_b", [128, 1], BF16)
    inp("ones_rf", [1, 128]); inp("ones_rb", [1, 128], BF16)
    inp("ident_b", [128, 128], BF16); inp("ident_f", [128, 128])
    inp("eps_c", [128, 1]); inp("qsc_c", [128, 2])
    NCONST = 13

    yT = nc.dram_tensor("yT", [DM, TPC], F32, kind="ExternalOutput").ap()
    dbg = {}
    if "qkv" in debug:
        for nm in ("dbgQ", "dbgK", "dbgV"):
            dbg[nm] = nc.dram_tensor(nm, [256, SEQ], BF16, kind="ExternalOutput").ap()
    if "stats" in debug:
        dbg["dbgST"] = nc.dram_tensor("dbgST", [2, SEQ], BF16, kind="ExternalOutput").ap()
    if "ao" in debug:
        dbg["dbgAO"] = nc.dram_tensor("dbgAO", [2 * DM, QT], BF16, kind="ExternalOutput").ap()

    st_in = nc.dram_tensor("st_in", [2, TPC], F32, kind="Internal").ap()
    st_out = nc.dram_tensor("st_out", [2 * NC, TPC], F32, kind="Internal").ap()
    stats_d = nc.dram_tensor("stats_d", [2, SEQ], BF16, kind="Internal").ap()
    wv_d = nc.dram_tensor("wv_d", [2, 128], BF16, kind="Internal").ap()
    a2a_in = [nc.dram_tensor(f"a2a_in{k}", [DM, QT], BF16, kind="Internal").ap() for k in range(2)]
    a2a_out = [nc.dram_tensor(f"a2a_out{k}", [DM, QT], BF16, kind="Internal").ap() for k in range(2)]

    es = ExitStack()
    with es:
        o_colb = es.enter_context(nc.sbuf_tensor("o_colb", [128, 1], BF16))
        o_negb = es.enter_context(nc.sbuf_tensor("o_negb", [128, 1], BF16))
        o_rowf = es.enter_context(nc.sbuf_tensor("o_rowf", [1, 128], F32))
        o_rowb = es.enter_context(nc.sbuf_tensor("o_rowb", [1, 128], BF16))
        eps_sb = es.enter_context(nc.sbuf_tensor("eps_sb", [128, 1], F32))
        qsc_sb = es.enter_context(nc.sbuf_tensor("qsc_sb", [128, 2], F32))
        bq_sb = es.enter_context(nc.sbuf_tensor("bq_sb", [128, 6], F32))
        sc_sb = es.enter_context(nc.sbuf_tensor("sc_sb", [128, NCH], F32))
        fbt_sb = es.enter_context(nc.sbuf_tensor("fbt_sb", [128, NCH], F32))
        bot_sb = es.enter_context(nc.sbuf_tensor("bot_sb", [128, NCH], F32))
        esAB = ExitStack()
        Qb = [esAB.enter_context(nc.sbuf_tensor(f"Qb{s}", [128, SEQ], BF16)) for s in (0, 1)]
        Kb = [esAB.enter_context(nc.sbuf_tensor(f"Kb{s}", [128, SEQ], BF16)) for s in (0, 1)]
        Vtb = [esAB.enter_context(nc.sbuf_tensor(f"Vt{s}", [128, SEQ], BF16)) for s in (0, 1)]
        btb = esAB.enter_context(nc.sbuf_tensor("btb", [128, NITER], F32))
        lnrT = esAB.enter_context(nc.sbuf_tensor("lnrT", [128, NKP], F32))
        rinv_sb = esAB.enter_context(nc.sbuf_tensor("rinv_sb", [128, NKP], BF16))
        colW = [esAB.enter_context(nc.sbuf_tensor(f"colW{i}", [128, NKP], F32)) for i in range(2)]
        tb_sb = esAB.enter_context(nc.sbuf_tensor("tb_sb", [128, QT], F32))
        esA = ExitStack()
        Wfj = esA.enter_context(nc.sbuf_tensor("Wfj", [128, NCH * 768], BF16))
        xs_sb = [esA.enter_context(nc.sbuf_tensor(f"xs{i}", [128, NCH * QT], BF16)) for i in range(2)]
        bqs_sb = esA.enter_context(nc.sbuf_tensor("bqs_sb", [128, 6], F32))
        w1_sb = esA.enter_context(nc.sbuf_tensor("w1_sb", [128, 6], F32))
        idn = esA.enter_context(nc.sbuf_tensor("idn", [128, 128], BF16))
        idf = esA.enter_context(nc.sbuf_tensor("idf", [128, 128], F32))
        r8 = esA.enter_context(nc.sbuf_tensor("r8", [NC, TPC], F32))
        mu8 = esA.enter_context(nc.sbuf_tensor("mu8", [NC, TPC], F32))
        rT_sb = esA.enter_context(nc.sbuf_tensor("rT_sb", [128, NKP], F32))
        muT_sb = esA.enter_context(nc.sbuf_tensor("muT_sb", [128, NKP], F32))
        rinvf = esA.enter_context(nc.sbuf_tensor("rinvf", [128, NKP], F32))
        w1vnb = [esA.enter_context(nc.sbuf_tensor(f"w1vnb{i}", [128, 128], BF16)) for i in range(2)]
        w1colb = [esA.enter_context(nc.sbuf_tensor(f"w1colb{i}", [128, 1], BF16)) for i in range(2)]
        r8b = esA.enter_context(nc.sbuf_tensor("r8b", [NC, TPC], BF16))
        m8b = esA.enter_context(nc.sbuf_tensor("m8b", [NC, TPC], BF16))
        esA1 = ExitStack()
        stx = [esA1.enter_context(nc.sbuf_tensor(f"stx{i}", [128, TPC], BF16)) for i in range(2)]
        stq = esA1.enter_context(nc.sbuf_tensor("stq", [128, TPC], BF16))
        sums8 = esA1.enter_context(nc.sbuf_tensor("sums8", [NC, TPC], F32))
        sumsq8 = esA1.enter_context(nc.sbuf_tensor("sumsq8", [NC, TPC], F32))
        tmp8 = esA1.enter_context(nc.sbuf_tensor("tmp8", [NC, TPC], F32))
        esA1.close()
        c1_sb = [esA.enter_context(nc.sbuf_tensor(f"c1_{i}", [128, QT], F32)) for i in range(2)]
        ct_sb = [esA.enter_context(nc.sbuf_tensor("ct_0", [128, QT], F32))]
        rBb = [esA.enter_context(nc.sbuf_tensor(f"rBb{i}", [128, QT], BF16)) for i in range(2)]
        mrBb = [esA.enter_context(nc.sbuf_tensor(f"mrBb{i}", [128, QT], BF16)) for i in range(2)]
        esA.enter_context(nc.sbuf_tensor("a2pad", [128, 4096], BF16))
        vstage = [[esA.enter_context(nc.sbuf_tensor(f"vst{v}_{i}", [128, QT], BF16))
                   for i in range(2)] for v in range(2)]
        esA.close()
        esB = ExitStack()
        Dsb = esB.enter_context(nc.sbuf_tensor("Dsb", [128, 10 * QT], BF16))
        E_sb = [esB.enter_context(nc.sbuf_tensor(f"E{i}", [128, QT], BF16)) for i in range(4)]
        P_sb = [esB.enter_context(nc.sbuf_tensor(f"P{i}", [128, QT], BF16)) for i in range(4)]
        rec_sb = esB.enter_context(nc.sbuf_tensor("rec_sb", [1, QT], F32))
        lnt_sb = esB.enter_context(nc.sbuf_tensor("lnt_sb", [1, QT], F32))
        rb_sb = esB.enter_context(nc.sbuf_tensor("rb_sb", [128, QT], F32))
        ao_sb = [esB.enter_context(nc.sbuf_tensor(f"ao{i}", [128, QT], BF16)) for i in range(2)]
        esB.close()
        esAB.close()
        esC = ExitStack()
        woS = [esC.enter_context(nc.sbuf_tensor(f"woS{i}", [128, NCH * 128], BF16)) for i in range(2)]
        Wf2_sb = esC.enter_context(nc.sbuf_tensor("Wf2_sb", [128, NCH * DM], BF16))
        x2_sb = esC.enter_context(nc.sbuf_tensor("x2_sb", [128, NCH * TPC], F32))
        xsl_sb = [esC.enter_context(nc.sbuf_tensor(f"xsl{i}", [128, QT], F32)) for i in range(2)]
        _t2 = esC.enter_context(nc.sbuf_tensor("t2_0", [128, QT], F32))
        t2_sb = [_t2, _t2]
        out_sb = [esC.enter_context(nc.sbuf_tensor(f"out{i}", [128, QT], F32)) for i in range(2)]
        xb2 = [esC.enter_context(nc.sbuf_tensor(f"xb2_{i}", [128, QT], BF16)) for i in range(2)]
        xq2 = [esC.enter_context(nc.sbuf_tensor(f"xq2_{i}", [128, QT], BF16)) for i in range(2)]
        mu2_sb = [esC.enter_context(nc.sbuf_tensor(f"mu2_{k}", [1, QT], F32)) for k in range(2)]
        vr2_sb = [esC.enter_context(nc.sbuf_tensor(f"vr2_{k}", [1, QT], F32)) for k in range(2)]
        r2_sb = [esC.enter_context(nc.sbuf_tensor(f"r2_{k}", [1, QT], F32)) for k in range(2)]
        r2B = [esC.enter_context(nc.sbuf_tensor(f"r2B{k}", [128, QT], F32)) for k in range(2)]
        m2B = [esC.enter_context(nc.sbuf_tensor(f"m2B{k}", [128, QT], F32)) for k in range(2)]
        esC1 = ExitStack()
        am_sb = [esC1.enter_context(nc.sbuf_tensor(f"am{k}", [128, NCH * QT], BF16)) for k in range(2)]
        esC1.close()
        xn_sb = [esC.enter_context(nc.sbuf_tensor(f"xn{k}", [128, NCH * QT], BF16)) for k in range(2)]
        esC.close()

        esP1 = ExitStack()
        pst = [esP1.enter_context(nc.psum_tensor(f"pst{i}", [1, QT], F32)) for i in range(4)]
        esP1.close()
        esPA = ExitStack()
        pw1 = esPA.enter_context(nc.psum_tensor("pw1", [128, 6], F32))
        prT = esPA.enter_context(nc.psum_tensor("prT", [128, 128], F32))
        esPA.enter_context(nc.psum_tensor("pdum0", [1, 4], F32))
        esPA.enter_context(nc.psum_tensor("pdum1", [1, 4], F32))
        pacc = [esPA.enter_context(nc.psum_tensor(f"pacc{i}", [128, QT], F32)) for i in range(2)]
        ptr = [esPA.enter_context(nc.psum_tensor(f"ptr{i}", [128, QT], BF16)) for i in range(2)]
        esPA.close()
        esPB = ExitStack()
        pS = [esPB.enter_context(nc.psum_tensor(f"pS{i}", [128, QT], F32)) for i in range(3)]
        pO = [esPB.enter_context(nc.psum_tensor(f"pO{i}", [128, QT], F32)) for i in range(2)]
        pSum = [esPB.enter_context(nc.psum_tensor(f"pSum{i}", [1, QT], F32)) for i in range(2)]
        pBc = esPB.enter_context(nc.psum_tensor("pBc", [128, QT], F32))
        esPB.close()
        esPC1 = ExitStack()
        pWo = [esPC1.enter_context(nc.psum_tensor(f"pWo{i}", [128, QT], F32)) for i in range(2)]
        pL = [esPC1.enter_context(nc.psum_tensor(f"pL{i}", [1, QT], F32)) for i in range(4)]
        esPC1.close()
        esPC2 = ExitStack()
        pF = [esPC2.enter_context(nc.psum_tensor(f"pF{i}", [128, QT], F32)) for i in range(2)]
        pB2 = [esPC2.enter_context(nc.psum_tensor(f"pB2_{i}", [128, QT], F32)) for i in range(2)]
        esPC2.close()

        sem = {}
        for nm in ("d_sq", "p_st", "d_stv", "s_stin", "ccS", "ccE", "ccO",
                   "s_gout", "d_mu", "s_rx", "d_mr", "s_rr", "s_std",
                   "g_w", "g_d", "p_w1", "d_w1", "d_r8b", "p_end", "g_wob", "g_wfb",
                   "p_bc", "d_bc", "p_fc", "s_fc", "g_ct", "d_corr",
                   "p_tr", "d_tr", "fA",
                   "d_rv", "d_wv", "s_wv", "p_rT", "d_rT", "s_lnr", "d_cw", "d_vb",
                   "p_S", "s_E", "d_P", "p_PV", "d_rec", "p_rb", "d_ao",
                   "fB",
                   "g_am0", "g_am1", "p_wo", "d_x2",
                   "d_xb", "p_ln", "s_mu2", "d_v2", "s_r2", "d_m2", "p_b2",
                   "d_b2", "d_xn", "p_ff", "d_out", "dbg",
                   "chV", "chS", "chG"):
            sem[nm] = es.enter_context(nc.semaphore(nm))
        for nm in ("g_stx", "g_x", "s_ao", "s_xsl", "s_out", "g_wo", "g_wf", "s_slice"):
            sem[nm] = [es.enter_context(nc.semaphore(f"{nm}{i}")) for i in range(2)]

        block = es.enter_context(nc.Block())

        def d_col(s, tau, kp):
            v = 0 if kp < 4 * tau else 1 + (kp - 4 * tau)
            return (s * 5 + v) * QT

        def qk_dst(tt, fc):
            if fc < 2:
                return Qb[fc][:, tt * QT:(tt + 1) * QT]
            return Kb[fc - 2][:, tt * QT:(tt + 1) * QT]

        @block.gpsimd
        def _(g):
            G = _Chain(g, sem["chG"])

            def stx_load(ch):
                if ch >= 2:
                    g.wait_ge(sem["p_st"], ch - 1)
                g.dma_start(out=stx[ch % 2][:],
                            in_=din["xstat"][ch * 128:(ch + 1) * 128, :]
                            ).then_inc(sem["g_stx"][ch % 2], 16)

            def x_load(tt):
                if tt >= 2:
                    g.wait_ge(sem["p_fc"], 6 * (tt - 1))
                g.dma_start(out=xs_sb[tt % 2][:].rearrange("p (n u) -> p n u", n=NCH),
                            in_=din["xT"][:, tt * QT:(tt + 1) * QT]
                                .rearrange("(n p) u -> p n u", p=128)
                            ).then_inc(sem["g_x"][tt % 2], 16)

            stx_load(0); stx_load(1)
            g.dma_start(out=Wfj[:].rearrange("p (n f) -> p n f", n=NCH),
                        in_=din["Wsl"].rearrange("(n p) f -> p n f", p=128)
                        ).then_inc(sem["g_w"], 16)
            x_load(0); x_load(1)
            for ch in range(2, NCH):
                stx_load(ch)
            for tt in range(2, NQT):
                x_load(tt)
                if tt == 4:
                    g.wait_ge(sem["s_stin"], 32)
                    g.collective_compute("AllGather", ALU.bypass,
                                         replica_groups=[list(range(NC))],
                                         ins=[st_in], outs=[st_out]).then_inc(sem["ccS"], 1)
            g.sem_inc(sem["fA"], 1)
            g.wait_ge(sem["p_fc"], 6 * NQT)
            g.dma_start(out=Dsb[:], in_=din["Dt"]).then_inc(sem["g_d"], 16)
            g.wait_ge(sem["fA"], 5)
            g.wait_ge(sem["s_ao"][0], 16 * (NWIN_EVEN // 2))
            g.wait_ge(sem["s_ao"][1], 16 * (NWIN_EVEN // 2))
            g.collective_compute("AllToAll", ALU.bypass,
                                 replica_groups=[list(range(NC))],
                                 ins=[a2a_in[0]], outs=[a2a_out[0]]).then_inc(sem["ccE"], 1)
            g.wait_ge(sem["ccE"], 1)
            g.dma_start(out=am_sb[0][:].rearrange("p (n u) -> p n u", n=NCH),
                        in_=a2a_out[0].rearrange("(n p) u -> p n u", p=128)
                        ).then_inc(sem["g_am0"], 16)
            g.sem_inc(sem["fB"], 1)
            w01 = max(w for w, win in enumerate(WINDOWS)
                      if win["s"] == 0 and win["tau"] <= 7)

            def woS_load(j):
                if j >= 2:
                    g.wait_ge(sem["p_wo"], j - 1)
                g.dma_start(out=woS[j % 2][:].rearrange("p (n f) -> p n f", n=NCH),
                            in_=din["Wo_p"][:, (j % NCH) * 128:(j % NCH + 1) * 128]
                                .rearrange("(n p) f -> p n f", p=128)
                            ).then_inc(sem["g_wo"][j % 2], 16)

            def wf_chunk(c):
                g.dma_start(out=Wf2_sb[:, c * 4 * DM:(c + 1) * 4 * DM]
                            .rearrange("p (n f) -> p n f", n=4),
                            in_=din["WfT"][c * 512:(c + 1) * 512, :]
                            .rearrange("(n p) f -> p n f", p=128)
                            ).then_inc(sem["g_wf"][0], 16)

            g.wait_ge(sem["d_ao"], w01 + 1)
            woS_load(0)
            woS_load(1)
            g.wait_ge(sem["p_end"], 1)
            wf_chunk(0)
            woS_load(2)
            woS_load(3)
            woS_load(4)
            g.wait_ge(sem["s_ao"][0], 16 * (NWIN // 2))
            g.wait_ge(sem["s_ao"][1], 16 * (NWIN // 2))
            g.collective_compute("AllToAll", ALU.bypass,
                                 replica_groups=[list(range(NC))],
                                 ins=[a2a_in[1]], outs=[a2a_out[1]]).then_inc(sem["ccO"], 1)
            for j in range(5, 2 * NCH):
                if j == 17:
                    g.wait_ge(sem["ccO"], 1)
                    g.dma_start(out=am_sb[1][:].rearrange("p (n u) -> p n u", n=NCH),
                                in_=a2a_out[1].rearrange("(n p) u -> p n u", p=128)
                                ).then_inc(sem["g_am1"], 16)
                woS_load(j)
                if j in (12, 19, 26):
                    c = (12, 19, 26).index(j) + 1
                    wf_chunk(c)

        @block.sync
        def _(sy):
            for nm, dst in (("btbl", btb), ("ones_b", o_colb), ("oneneg_b", o_negb),
                            ("ones_rf", o_rowf), ("ones_rb", o_rowb),
                            ("ident_b", idn), ("ident_f", idf), ("bsl", bq_sb),
                            ("sc_t", sc_sb), ("fb_t", fbt_sb), ("bo_t", bot_sb),
                            ("eps_c", eps_sb), ("qsc_c", qsc_sb)):
                sy.dma_start(out=dst[:], in_=din[nm]).then_inc(sem["s_std"], 16)
            sy.wait_ge(sem["d_stv"], 1)
            sy.dma_start(out=st_in[0:1, :], in_=sums8[0:1, :]).then_inc(sem["s_stin"], 16)
            sy.dma_start(out=st_in[1:2, :], in_=sumsq8[0:1, :]).then_inc(sem["s_stin"], 16)
            sy.wait_ge(sem["ccS"], 1)
            sy.dma_start(out=sums8[:].rearrange("p (o t) -> p o t", o=1),
                         in_=st_out.rearrange("(n a) t -> n a t", a=2)[:, 0:1, :]
                         ).then_inc(sem["s_gout"], 16)
            sy.dma_start(out=sumsq8[:].rearrange("p (o t) -> p o t", o=1),
                         in_=st_out.rearrange("(n a) t -> n a t", a=2)[:, 1:2, :]
                         ).then_inc(sem["s_gout"], 16)
            sy.wait_ge(sem["d_r8b"], 1)
            sy.dma_start(out=stats_d[0:1, :].rearrange("a (p t) -> (a p) t", p=NC),
                         in_=r8b[:]).then_inc(sem["s_rr"], 16)
            sy.dma_start(out=stats_d[1:2, :].rearrange("a (p t) -> (a p) t", p=NC),
                         in_=m8b[:]).then_inc(sem["s_rr"], 16)
            for vs in range(2):
                sy.wait_ge(sem["d_wv"], vs + 1)
                sy.dma_start(out=wv_d[vs].rearrange("(k b) -> k b", b=1),
                             in_=w1colb[vs][:]).then_inc(sem["s_wv"], 16)
                sy.wait_ge(sem["s_wv"], 32 * vs + 16)
                sy.dma_start(out=w1vnb[vs][:],
                             in_=wv_d[vs:vs + 1, :].partition_broadcast(128)
                             ).then_inc(sem["s_wv"], 16)
            sy.wait_ge(sem["s_rr"], 32)
            for bt in range(NQT):
                if bt >= 2:
                    sy.wait_ge(sem["d_corr"], 4 * bt - 4)
                sy.dma_start(out=rBb[bt % 2][:],
                             in_=stats_d[0:1, bt * QT:(bt + 1) * QT]
                                 .partition_broadcast(128)
                             ).then_inc(sem["s_slice"][bt % 2], 16)
                sy.dma_start(out=mrBb[bt % 2][:],
                             in_=stats_d[1:2, bt * QT:(bt + 1) * QT]
                                 .partition_broadcast(128)
                             ).then_inc(sem["s_slice"][bt % 2], 16)
            if "stats" in debug:
                sy.wait_ge(sem["s_rr"], 32)
                sy.dma_start(out=dbg["dbgST"], in_=stats_d).then_inc(sem["dbg"], 16)
            if "qkv" in debug:
                sy.wait_ge(sem["d_corr"], 4 * NQT)
                sy.wait_ge(sem["d_vb"], 2 * NKP)
                for s in (0, 1):
                    sy.dma_start(out=dbg["dbgQ"][s * 128:(s + 1) * 128, :],
                                 in_=Qb[s][:]).then_inc(sem["dbg"], 16)
                    sy.dma_start(out=dbg["dbgK"][s * 128:(s + 1) * 128, :],
                                 in_=Kb[s][:]).then_inc(sem["dbg"], 16)
                    sy.dma_start(out=dbg["dbgV"][s * 128:(s + 1) * 128, :],
                                 in_=Vtb[s][:]).then_inc(sem["dbg"], 16)
            sy.sem_inc(sem["fA"], 1)
            for w, win in enumerate(WINDOWS):
                sy.wait_ge(sem["d_ao"], w + 1)
                cons, half = win["tau"] // 2, win["tau"] % 2
                r0 = cons * 256 + win["s"] * 128
                sy.dma_start(out=a2a_in[half][r0:r0 + 128, :],
                             in_=ao_sb[w % 2][:]).then_inc(sem["s_ao"][w % 2], 16)
            if "ao" in debug:
                for half in range(2):
                    sy.dma_start(out=dbg["dbgAO"][half * DM:(half + 1) * DM, :],
                                 in_=a2a_in[half]).then_inc(sem["dbg"], 16)
            sy.sem_inc(sem["fB"], 1)
            for m in range(2 * NCH):
                k, oc = m // NCH, m % NCH
                if m >= 2:
                    sy.wait_ge(sem["d_x2"], m - 1)
                sy.dma_start(out=xsl_sb[m % 2][:],
                             in_=din["xstat"][oc * 128:(oc + 1) * 128,
                                              k * QT:(k + 1) * QT]
                             ).then_inc(sem["s_xsl"][m % 2], 16)
            for k in range(2):
                for oc in range(NCH):
                    m = NCH * k + oc
                    sy.wait_ge(sem["d_out"], m + 1)
                    sy.dma_start(out=yT[oc * 128:(oc + 1) * 128,
                                        k * QT:(k + 1) * QT],
                                 in_=out_sb[m % 2][:]).then_inc(sem["s_out"][m % 2], 16)
            sy.wait_ge(sem["s_out"][0], 16 * NCH)
            sy.wait_ge(sem["s_out"][1], 16 * NCH)

        @block.tensor
        def _(t):
            t.wait_ge(sem["s_std"], 16 * NCONST)

            def stats_ch(ch):
                t.wait_ge(sem["g_stx"][ch % 2], 16 * (ch // 2 + 1))
                t.wait_ge(sem["d_sq"], ch + 1)
                for half in range(2):
                    c0 = half * QT
                    t.matmul(pst[half][:], o_colb[:], stx[ch % 2][:, c0:c0 + QT],
                             start=(ch == 0), stop=(ch == NCH - 1))
                    mm = t.matmul(pst[2 + half][:], o_colb[:], stq[:, c0:c0 + QT],
                                  start=(ch == 0), stop=(ch == NCH - 1))
                    if half == 1:
                        mm.then_inc(sem["p_st"], 1)

            def w1_group():
                t.wait_ge(sem["g_w"], 16)
                t.wait_ge(sem["d_stv"], 1)
                for fc in range(6):
                    for ic in range(NCH):
                        mm = t.matmul(pw1[:, fc:fc + 1],
                                      Wfj[:, ic * 768 + fc * 128:ic * 768 + fc * 128 + 128],
                                      o_negb[:], start=(fc == 0 and ic == 0),
                                      stop=(fc == 5 and ic == NCH - 1))
                        if fc == 5 and ic == NCH - 1:
                            mm.then_inc(sem["p_w1"], 1)

            stats_ch(0)
            stats_ch(1)
            t.wait_ge(sem["g_w"], 16)

            def v_transposes(ti):
                for vs in range(2):
                    t.wait_ge(sem["s_fc"], 6 * ti + 5 + vs)
                    if ti >= 1:
                        t.wait_ge(sem["d_tr"], 2 * (ti - 1) + vs + 1)
                    for sub in range(4):
                        mm = t.transpose(ptr[vs][:, sub * 128:(sub + 1) * 128],
                                         vstage[vs][ti % 2][:, sub * 128:(sub + 1) * 128],
                                         idn[:])
                    mm.then_inc(sem["p_tr"], 1)

            def row_transposes():
                t.wait_ge(sem["d_mr"], 1)
                for gi, src in enumerate((r8, mu8)):
                    for c in range(8):
                        n = 8 * gi + c
                        if n >= 1:
                            t.wait_ge(sem["d_rT"], n)
                        t.transpose(prT[:, 0:8], src[:, 128 * c:128 * (c + 1)],
                                    idf[0:8, 0:8]).then_inc(sem["p_rT"], 1)

            for tt in range(NQT):
                if tt >= 1:
                    v_transposes(tt - 1)
                if tt == 1:
                    stats_ch(14)
                    stats_ch(15)
                    w1_group()
                if tt == BCLAG:
                    row_transposes()
                t.wait_ge(sem["g_x"][tt % 2], 16 * (tt // 2 + 1))
                for fc in range(6):
                    idx = 6 * tt + fc
                    if tt == 0 and fc >= 1:
                        stats_ch(2 * fc)
                        stats_ch(2 * fc + 1)
                    if tt == 0 and fc == 5:
                        stats_ch(12)
                        stats_ch(13)
                    if idx >= 2:
                        t.wait_ge(sem["s_fc"], idx - 1)
                    for ic in range(NCH):
                        mm = t.matmul(pacc[idx % 2][:],
                                      Wfj[:, ic * 768 + fc * 128:ic * 768 + fc * 128 + 128],
                                      xs_sb[tt % 2][:, ic * QT:(ic + 1) * QT],
                                      start=(ic == 0), stop=(ic == NCH - 1))
                        if ic == NCH - 1:
                            mm.then_inc(sem["p_fc"], 1)
            v_transposes(NQT - 1)
            t.sem_inc(sem["fA"], 1)
            t.wait_ge(sem["g_d"], 16)
            vbmap = {}
            _nvb = 0
            V_AT_PE = {6: (0,), 7: (1, 2), 8: (3, 4), 9: (5, 6), 10: (7, 8),
                       11: (9, 10), 12: (11,), 13: (12,), 14: (13,), 15: (14, 15)}
            for _tt in range(NQT):
                for _ti in V_AT_PE.get(_tt, ()):
                    for _vs in range(2):
                        for _kp in range(4 * _ti, 4 * _ti + 4):
                            _nvb += 1
                            vbmap[(_vs, _kp)] = _nvb
            cur_corr = [0]
            cur_vb = [0]

            def gate_corr(need):
                if need > cur_corr[0]:
                    t.wait_ge(sem["d_corr"], need)
                    cur_corr[0] = need

            def gate_vb(need):
                if need > cur_vb[0]:
                    t.wait_ge(sem["d_vb"], need)
                    cur_vb[0] = need
            bc_at = {}
            for w, win in enumerate(WINDOWS):
                bc_at.setdefault(win["it0"] + win["n"] - 1 + APD + 2, []).append(w)
            for ii in range(NITER + APD):
                if ii < NITER:
                    for w in bc_at.get(ii, []):
                        t.wait_ge(sem["d_rec"], w + 1)
                        t.matmul(pBc[:], o_rowf[:], rec_sb[:], start=True, stop=True
                                 ).then_inc(sem["p_rb"], 1)
                    it = ITERS[ii]
                    if ii == 1:
                        t.wait_ge(sem["d_rT"], 16)
                    if ii >= 3:
                        t.wait_ge(sem["s_E"], ii - 2)
                    gate_corr(4 * it["tau"] + it["s"] + 1)
                    gate_corr(4 * (it["kp"] // 4) + 3 + it["s"])
                    mm = t.matmul(pS[ii % 3][:],
                             Kb[it["s"]][:, it["kp"] * KT:(it["kp"] + 1) * KT],
                             Qb[it["s"]][:, it["tau"] * QT:(it["tau"] + 1) * QT],
                             start=True, stop=True)
                    mm.then_inc(sem["p_S"], 1)
                jj = ii - APD
                if 0 <= jj < NITER:
                    it2 = ITERS[jj]
                    if jj == 0:
                        t.wait_ge(sem["d_rv"], 1)
                        t.wait_ge(sem["s_fc"], 6 * NQT)
                        t.wait_ge(sem["d_tr"], 2 * NQT)
                    if it2["first"] and it2["w"] >= 2:
                        t.wait_ge(sem["d_ao"], it2["w"] - 1)
                    gate_vb(vbmap[(it2["s"], it2["kp"])])
                    t.wait_ge(sem["d_P"], jj + 1)
                    ow = it2["w"] % 2
                    t.matmul(pO[ow][:],
                             Vtb[it2["s"]][:, it2["kp"] * KT:(it2["kp"] + 1) * KT],
                             P_sb[jj % 4][:],
                             start=it2["first"], stop=it2["last"])
                    t.matmul(pSum[ow][:], rinv_sb[:, it2["kp"]:it2["kp"] + 1],
                             P_sb[jj % 4][:],
                             start=it2["first"], stop=it2["last"]
                             ).then_inc(sem["p_PV"], 1)
            for w_list in [bc_at.get(s2, []) for s2 in range(NITER, NITER + APD + 3)]:
                for w in w_list:
                    t.wait_ge(sem["d_rec"], w + 1)
                    t.matmul(pBc[:], o_rowf[:], rec_sb[:], start=True, stop=True
                             ).then_inc(sem["p_rb"], 1)
            t.sem_inc(sem["p_end"], 1)
            t.sem_inc(sem["fB"], 1)
            t.wait_ge(sem["s_E"], NITER)
            t.wait_ge(sem["d_ao"], NWIN)
            def ln2_mms(m):
                k2, oc2 = m // NCH, m % NCH
                t.wait_ge(sem["d_xb"], m + 1)
                t.matmul(pL[2 * k2][:], o_colb[:], xb2[m % 2][:],
                         start=(oc2 == 0), stop=(oc2 == NCH - 1))
                t.matmul(pL[2 * k2 + 1][:], o_colb[:], xq2[m % 2][:],
                         start=(oc2 == 0), stop=(oc2 == NCH - 1)
                         ).then_inc(sem["p_ln"], 1)

            for k in range(2):
                t.wait_ge(sem["g_am0" if k == 0 else "g_am1"], 16)
                for oc in range(NCH):
                    m = NCH * k + oc
                    t.wait_ge(sem["g_wo"][m % 2], 16 * (m // 2 + 1))
                    if m >= 2:
                        t.wait_ge(sem["d_x2"], m - 1)
                    for ic in range(NCH):
                        mm = t.matmul(pWo[m % 2][:],
                                      woS[m % 2][:, ic * 128:(ic + 1) * 128],
                                      am_sb[k][:, ic * QT:(ic + 1) * QT],
                                      start=(ic == 0), stop=(ic == NCH - 1))
                        if ic == NCH - 1:
                            mm.then_inc(sem["p_wo"], 1)
                    if m >= 1:
                        ln2_mms(m - 1)
                    if k == 1 and oc == 1:
                        t.wait_ge(sem["d_m2"], 1)
                        t.matmul(pB2[0][:], o_rowf[:], r2_sb[0][:], start=True, stop=True)
                        t.matmul(pB2[1][:], o_rowf[:], mu2_sb[0][:], start=True, stop=True
                                 ).then_inc(sem["p_b2"], 1)
            ln2_mms(2 * NCH - 1)
            t.wait_ge(sem["g_wf"][0], 64)
            t.wait_ge(sem["d_x2"], 32)
            for k in range(2):
                for oc in range(NCH):
                    m = NCH * k + oc
                    if k == 0 and oc == 2:
                        t.wait_ge(sem["d_m2"], 2)
                        t.wait_ge(sem["d_b2"], 1)
                        t.matmul(pB2[0][:], o_rowf[:], r2_sb[1][:], start=True, stop=True)
                        t.matmul(pB2[1][:], o_rowf[:], mu2_sb[1][:], start=True, stop=True
                                 ).then_inc(sem["p_b2"], 1)
                    t.wait_ge(sem["d_xn"], 16 * (k + 1))
                    if m >= 2:
                        t.wait_ge(sem["d_out"], m - 1)
                    for ic in range(NCH):
                        mm = t.matmul(pF[m % 2][:],
                                      Wf2_sb[:, ic * DM + oc * 128:ic * DM + oc * 128 + 128],
                                      xn_sb[k][:, ic * QT:(ic + 1) * QT],
                                      start=(ic == 0), stop=(ic == NCH - 1))
                        if ic == NCH - 1:
                            mm.then_inc(sem["p_ff"], 1)

        @block.scalar
        def _(s_):
            S = _Chain(s_, sem["chS"])
            s_.wait_ge(sem["s_std"], 16 * NCONST)
            for tt in range(NQT):
                for fc in range(6):
                    idx = 6 * tt + fc
                    s_.wait_ge(sem["p_fc"], idx + 1)
                    if fc < 4:
                        dst = qk_dst(tt, fc)
                    else:
                        vs = fc - 4
                        if tt >= 2:
                            s_.wait_ge(sem["p_tr"], 2 * (tt - 2) + vs + 1)
                        dst = vstage[vs][tt % 2][:]
                    S(lambda: s_.activation(dst, pacc[idx % 2][:], AF.Copy),
                      inc=(sem["s_fc"], idx + 1, 1))
                if tt >= 1:
                    for vs in range(2):
                        tr = 2 * (tt - 1) + vs
                        s_.wait_ge(sem["p_tr"], tr + 1)
                        S(lambda: s_.activation(Vtb[vs][:, (tt - 1) * QT:tt * QT],
                                                ptr[vs][:], AF.Copy),
                          inc=(sem["d_tr"], tr + 1, 1))
                if tt == 4:
                    s_.wait_ge(sem["d_mu"], 1)
                    S(lambda: s_.activation(sumsq8[:], tmp8[:], AF.Ln,
                                            bias=eps_sb[0:NC, :]))
                    S(lambda: s_.activation(r8[:], sumsq8[:], AF.Exp, scale=-0.5),
                      inc=(sem["s_rx"], 1, 1))
            for vs in range(2):
                tr = 2 * (NQT - 1) + vs
                s_.wait_ge(sem["p_tr"], tr + 1)
                S(lambda: s_.activation(Vtb[vs][:, (NQT - 1) * QT:NQT * QT],
                                        ptr[vs][:], AF.Copy),
                  inc=(sem["d_tr"], tr + 1, 1))
            s_.wait_ge(sem["d_rT"], 16)
            S(lambda: s_.activation(lnrT[:], rT_sb[:], AF.Ln),
              inc=(sem["s_lnr"], 1, 1))
            s_.sem_inc(sem["fA"], 1)
            s_.wait_ge(sem["fA"], 5)
            srec_at = {}
            for w, win in enumerate(WINDOWS):
                srec_at.setdefault(win["it0"] + win["n"] - 1 + APD, []).append(w)
            for i in range(NITER + APD + 1):
                for w in srec_at.get(i, []):
                    s_.wait_ge(sem["p_PV"], WINDOWS[w]["it0"] + WINDOWS[w]["n"])
                    S(lambda: s_.activation(lnt_sb[:], pSum[w % 2][:], AF.Ln))
                    S(lambda: s_.activation(rec_sb[:], lnt_sb[:], AF.Exp, scale=-1.0),
                      inc=(sem["d_rec"], w + 1, 1))
                if i < NITER:
                    it = ITERS[i]
                    if it["first"]:
                        s_.wait_ge(sem["d_cw"], it["w"] + 1)
                    s_.wait_ge(sem["p_S"], i + 1)
                    if i >= 4:
                        s_.wait_ge(sem["d_P"], i - 3)
                    S(lambda: s_.activation(E_sb[i % 4][:], pS[i % 3][:], AF.Exp,
                                            bias=colW[it["w"] % 2][:, it["j"]:it["j"] + 1],
                                            scale=1.0),
                      inc=(sem["s_E"], i + 1, 1))
            s_.sem_inc(sem["fB"], 1)
            s_.wait_ge(sem["fB"], 5)
            for k in range(2):
                s_.wait_ge(sem["p_ln"], NCH * (k + 1))
                S(lambda: s_.mul(mu2_sb[k][:], pL[2 * k][:], 1.0 / DM))
                S(lambda: s_.mul(vr2_sb[k][:], pL[2 * k + 1][:], 1.0 / DM),
                  inc=(sem["s_mu2"], k + 1, 1))
                s_.wait_ge(sem["d_v2"], k + 1)
                S(lambda: s_.activation(vr2_sb[k][:], vr2_sb[k][:], AF.Ln,
                                        bias=eps_sb[0:1, :]))
                S(lambda: s_.activation(r2_sb[k][:], vr2_sb[k][:], AF.Exp, scale=-0.5),
                  inc=(sem["s_r2"], k + 1, 1))

        @block.vector
        def _(v):
            V = _Chain(v, sem["chV"])
            for ch in range(NCH):
                v.wait_ge(sem["g_stx"][ch % 2], 16 * (ch // 2 + 1))
                if ch >= 1:
                    v.wait_ge(sem["p_st"], ch)
                V(lambda: v.tensor_mul(stq[:], stx[ch % 2][:], stx[ch % 2][:]),
                  inc=(sem["d_sq"], ch + 1, 1))
            v.wait_ge(sem["p_st"], NCH)
            V(lambda: v.tensor_copy(sums8[0:1, 0:QT], pst[0][:]))
            V(lambda: v.tensor_copy(sums8[0:1, QT:2 * QT], pst[1][:]))
            V(lambda: v.tensor_copy(sumsq8[0:1, 0:QT], pst[2][:]))
            V(lambda: v.tensor_copy(sumsq8[0:1, QT:2 * QT], pst[3][:]),
              inc=(sem["d_stv"], 1, 1))
            v.wait_ge(sem["p_w1"], 1)
            V(lambda: v.tensor_scalar_mul(w1_sb[:, 0:2], pw1[:, 0:2], RSC))
            V(lambda: v.tensor_copy(w1_sb[:, 2:6], pw1[:, 2:6]))
            V(lambda: v.tensor_scalar_mul(bqs_sb[:, 0:2], bq_sb[:, 0:2], RSC))
            V(lambda: v.tensor_copy(bqs_sb[:, 2:6], bq_sb[:, 2:6]),
              inc=(sem["d_w1"], 1, 1))
            v.wait_ge(sem["s_gout"], 32)
            V(lambda: v.tensor_scalar_mul(mu8[:], sums8[:], 1.0 / DM))
            V(lambda: v.tensor_scalar_mul(tmp8[:], sumsq8[:], 1.0 / DM))
            V(lambda: v.tensor_mul(sums8[:], mu8[:], mu8[:]))
            V(lambda: v.tensor_sub(tmp8[:], tmp8[:], sums8[:]),
              inc=(sem["d_mu"], 1, 1))
            v.wait_ge(sem["s_rx"], 1)
            V(lambda: v.tensor_mul(sums8[:], mu8[:], r8[:]),
              inc=(sem["d_mr"], 1, 1))
            V(lambda: v.tensor_copy(r8b[:], r8[:]))
            V(lambda: v.tensor_copy(m8b[:], sums8[:]),
              inc=(sem["d_r8b"], 1, 1))
            for vs in range(2):
                V(lambda: v.tensor_copy(w1colb[vs][:], w1_sb[:, 4 + vs:5 + vs]),
                  inc=(sem["d_wv"], vs + 1, 1))
            for gi, dstT in enumerate((rT_sb, muT_sb)):
                for c in range(8):
                    n = 8 * gi + c
                    v.wait_ge(sem["p_rT"], n + 1)
                    V(lambda: v.tensor_copy(
                        dstT[:].rearrange("p (a b) -> p a b", b=8)[:, :, c:c + 1],
                        prT[:, 0:8].rearrange("p (a b) -> p a b", b=1)),
                      inc=(sem["d_rT"], n + 1, 1))
            V(lambda: v.reciprocal(rinvf[:], rT_sb[:]))
            V(lambda: v.tensor_copy(rinv_sb[:], rinvf[:]),
              inc=(sem["d_rv"], 1, 1))
            v.wait_ge(sem["s_wv"], 64)
            V_AT = {6: (0,), 7: (1, 2), 8: (3, 4), 9: (5, 6), 10: (7, 8),
                    11: (9, 10), 12: (11,), 13: (12,), 14: (13,), 15: (14, 15)}
            nvb = 0
            for tt in range(NQT):
                v.wait_ge(sem["s_slice"][tt % 2], 32 * (tt // 2 + 1))
                for fc in range(4):
                    q = 4 * tt + fc
                    v.wait_ge(sem["s_fc"], 6 * tt + fc + 1)
                    dst = qk_dst(tt, fc)
                    V(lambda: v.tensor_scalar(ct_sb[0][:], mrBb[tt % 2][:],
                                              w1_sb[:, fc:fc + 1], bqs_sb[:, fc:fc + 1],
                                              ALU.mult, ALU.add))
                    V(lambda: v.tensor_mul(c1_sb[q % 2][:], dst, rBb[tt % 2][:]))
                    V(lambda: v.scalar_tensor_tensor(
                        dst, c1_sb[q % 2][:],
                        qsc_sb[:, (0 if fc < 2 else 1):(1 if fc < 2 else 2)],
                        ct_sb[0][:], ALU.mult, ALU.add),
                      inc=(sem["d_corr"], q + 1, 1))
                for ti in V_AT.get(tt, ()):
                    for vs in range(2):
                        v.wait_ge(sem["d_tr"], 2 * ti + vs + 1)
                        for kp in range(4 * ti, 4 * ti + 4):
                            sl2 = Vtb[vs][:, kp * KT:(kp + 1) * KT]
                            nvb += 1
                            V(lambda: v.scalar_tensor_tensor(
                                sl2, w1vnb[vs][:], muT_sb[:, kp:kp + 1],
                                sl2, ALU.mult, ALU.add),
                              inc=(sem["d_vb"], nvb, 1))
            v.sem_inc(sem["fA"], 1)
            v.wait_ge(sem["fA"], 5)
            v.wait_ge(sem["s_lnr"], 1)
            for w0 in (0, 1):
                win = WINDOWS[w0]
                V(lambda: v.tensor_add(colW[w0][:, 0:win["n"]],
                                       btb[:, win["it0"]:win["it0"] + win["n"]],
                                       lnrT[:, win["kpos"][0]:win["kpos"][0] + win["n"]]),
                  inc=(sem["d_cw"], w0 + 1, 1))
            cw_at, fin_at = {}, {}
            for w, win in enumerate(WINDOWS):
                if w >= 2:
                    cw_at.setdefault(WINDOWS[w - 1]["it0"], []).append(w)
                fin_at.setdefault(win["it0"] + win["n"] - 1 + APD + 2, []).append(w)
            for i in range(NITER + APD + 5):
                for w in cw_at.get(i, []):
                    win = WINDOWS[w]
                    v.wait_ge(sem["s_E"], WINDOWS[w - 1]["it0"])
                    V(lambda: v.tensor_add(colW[w % 2][:, 0:win["n"]],
                                           btb[:, win["it0"]:win["it0"] + win["n"]],
                                           lnrT[:, win["kpos"][0]:win["kpos"][0] + win["n"]]),
                      inc=(sem["d_cw"], w + 1, 1))
                for w in fin_at.get(i, []):
                    v.wait_ge(sem["p_rb"], w + 1)
                    V(lambda: v.tensor_copy(rb_sb[:], pBc[:]))
                    V(lambda: v.tensor_mul(tb_sb[:], pO[w % 2][:], rb_sb[:]))
                    if w >= 2:
                        v.wait_ge(sem["s_ao"][w % 2], 16 * (w // 2))
                    sv = WINDOWS[w]["s"]
                    V(lambda: v.tensor_scalar_add(ao_sb[w % 2][:], tb_sb[:],
                                                  bq_sb[:, 4 + sv:5 + sv]),
                      inc=(sem["d_ao"], w + 1, 1))
                if i < NITER:
                    it = ITERS[i]
                    v.wait_ge(sem["s_E"], i + 1)
                    if i >= 4:
                        v.wait_ge(sem["p_PV"], i - 3)
                    dc = d_col(it["s"], it["tau"], it["kp"])
                    V(lambda: v.tensor_mul(P_sb[i % 4][:], E_sb[i % 4][:],
                                           Dsb[:, dc:dc + QT]),
                      inc=(sem["d_P"], i + 1, 1))
            v.sem_inc(sem["fB"], 1)
            v.wait_ge(sem["fB"], 5)

            def x2xb(k, oc):
                m = NCH * k + oc
                v.wait_ge(sem["p_wo"], m + 1)
                v.wait_ge(sem["s_xsl"][m % 2], 16 * (m // 2 + 1))
                x2v = x2_sb[:, oc * TPC + k * QT:oc * TPC + (k + 1) * QT]
                V(lambda: v.scalar_tensor_tensor(x2v, pWo[m % 2][:],
                                                 bot_sb[:, oc:oc + 1], xsl_sb[m % 2][:],
                                                 ALU.add, ALU.add),
                  inc=(sem["d_x2"], m + 1, 1))
                if m >= 2:
                    v.wait_ge(sem["p_ln"], m - 1)
                V(lambda: v.tensor_copy(xb2[m % 2][:], x2v))
                V(lambda: v.tensor_mul(xq2[m % 2][:], xb2[m % 2][:], xb2[m % 2][:]),
                  inc=(sem["d_xb"], m + 1, 1))

            def ln2_stats(k):
                v.wait_ge(sem["s_mu2"], k + 1)
                V(lambda: v.tensor_mul(r2_sb[k][:], mu2_sb[k][:], mu2_sb[k][:]))
                V(lambda: v.tensor_sub(vr2_sb[k][:], vr2_sb[k][:], r2_sb[k][:]),
                  inc=(sem["d_v2"], k + 1, 1))
                v.wait_ge(sem["s_r2"], k + 1)
                V(lambda: v.tensor_mul(mu2_sb[k][:], mu2_sb[k][:], r2_sb[k][:]),
                  inc=(sem["d_m2"], k + 1, 1))

            def xn_op(k, ic):
                x2v = x2_sb[:, ic * TPC + k * QT:ic * TPC + (k + 1) * QT]
                xnv = xn_sb[k][:, ic * QT:(ic + 1) * QT]
                V(lambda: v.tensor_mul(xnv, x2v, r2B[k][:]))
                V(lambda: v.tensor_sub(xnv, xnv, m2B[k][:]),
                  inc=(sem["d_xn"], 16 * k + ic + 1, 1))

            for oc in range(NCH):
                x2xb(0, oc)
            ln2_stats(0)
            for oc in range(NCH):
                x2xb(1, oc)
                if oc == 2:
                    v.wait_ge(sem["p_b2"], 1)
                    V(lambda: v.tensor_copy(r2B[0][:], pB2[0][:]))
                    V(lambda: v.tensor_copy(m2B[0][:], pB2[1][:]),
                      inc=(sem["d_b2"], 1, 1))
                if 2 <= oc <= 9:
                    xn_op(0, 2 * (oc - 2))
                    xn_op(0, 2 * (oc - 2) + 1)
            ln2_stats(1)
            for k in range(2):
                for oc in range(NCH):
                    m = NCH * k + oc
                    v.wait_ge(sem["p_ff"], m + 1)
                    if m >= 2:
                        v.wait_ge(sem["s_out"][m % 2], 16 * (m // 2))
                    V(lambda: v.tensor_scalar(t2_sb[m % 2][:], pF[m % 2][:],
                                              sc_sb[:, oc:oc + 1], fbt_sb[:, oc:oc + 1],
                                              ALU.mult, ALU.add))
                    V(lambda: v.tensor_add(out_sb[m % 2][:], t2_sb[m % 2][:],
                                           x2_sb[:, oc * TPC + k * QT:oc * TPC + (k + 1) * QT]),
                      inc=(sem["d_out"], m + 1, 1))
                    if k == 0:
                        if oc == 0:
                            v.wait_ge(sem["p_b2"], 2)
                            V(lambda: v.tensor_copy(r2B[1][:], pB2[0][:]))
                            V(lambda: v.tensor_copy(m2B[1][:], pB2[1][:]),
                              inc=(sem["d_b2"], 2, 1))
                        xn_op(1, oc)

    return nc, dbg



def kernel(**inputs):
    in_maps = host_prepare(inputs)
    nc, _ = build_program()
    res = run_bass_kernel_spmd(nc, in_maps, core_ids=list(range(NC)))
    y = np.empty((SEQ, DM), dtype=np.float32)
    for c in range(NC):
        y[c * TPC:(c + 1) * TPC, :] = res.results[c]["yT"].T
    return y.reshape(1, SEQ, DM)
